# version 25
# baseline (speedup 1.0000x reference)
import math
import numpy as np
import concourse.bass as bass
import concourse.mybir as mybir
from concourse.bass_utils import run_bass_kernel_spmd

F32 = mybir.dt.float32
BF16 = mybir.dt.bfloat16
I32 = mybir.dt.int32
AF = mybir.ActivationFunctionType
ALU = mybir.AluOpType

D = 1024
NH = 8
DH = 128
HALF = 64
H2 = 3072
NC2 = H2 // 128
NK = D // 128
TILE = 128
GTILES = 8
GTOK = TILE * GTILES
SCALE = 1.0 / math.sqrt(DH)
LNSCALE = math.log(SCALE)
THETA = 10000.0
TWO_PI = 2.0 * math.pi
_c1 = np.float32(6.28125)
_c2r = np.float64(TWO_PI) - np.float64(_c1)
_c2 = np.frombuffer((np.frombuffer(np.float32(_c2r).tobytes(),
                                   dtype=np.uint32) &
                     np.uint32(0xFFFFF000)).tobytes(), dtype=np.float32)[0]
_c3 = np.float32(np.float64(TWO_PI) - np.float64(_c1) - np.float64(_c2))
C1, C2, C3 = float(_c1), float(_c2), float(_c3)


class _Prog:

    ENGINES = ("pe", "dve", "act", "gps", "sp")

    def __init__(self, nc):
        self.nc = nc
        self.q = {e: [] for e in self.ENGINES}
        self.sem = {}
        for name in ("s_pe", "s_dve", "s_act", "s_gpc", "s_sp"):
            self.sem[name] = [nc.alloc_semaphore(name), 0]
        self.esem = {"pe": "s_pe", "dve": "s_dve", "act": "s_act",
                     "gps": "s_gpc", "sp": "s_sp"}
        self.dring = {}
        for qn, n in (("gps", 10), ("sp", 8), ("act", 6)):
            for j in range(n):
                nm = f"s_d{qn}{j}"
                self.sem[nm] = [nc.alloc_semaphore(nm), 0]
            self.dring[qn] = (n, [0])
        self.buf = {}
        self.clock = {e: {} for e in self.ENGINES}
        self.barrier_tok = None
        self.barrier_pending = set()

    def op(self, eng, fn, reads=(), writes=(), dma=False, tag=""):
        waits = {}
        if dma:
            nring, cnt = self.dring[eng if eng != "actq" else "act"]
            j = cnt[0] % nring
            cnt[0] += 1
            sname = f"s_d{eng if eng != 'actq' else 'act'}{j}"
            amt = 16
            if self.sem[sname][1] > 0:
                waits[sname] = self.sem[sname][1]
        else:
            sname = self.esem[eng]
            amt = 1
        if eng == "actq":
            eng = "act"
        if eng in self.barrier_pending:
            self.barrier_pending.discard(eng)
            for s, v in self.barrier_tok.items():
                waits[s] = max(waits.get(s, 0), v)
        for b in reads:
            e = self.buf.get(b)
            if e and e["w"] is not None:
                s, v = e["w"]
                waits[s] = max(waits.get(s, 0), v)
        for b in writes:
            e = self.buf.get(b)
            if e:
                if e["w"] is not None:
                    s, v = e["w"]
                    waits[s] = max(waits.get(s, 0), v)
                for (s, v) in e["r"]:
                    waits[s] = max(waits.get(s, 0), v)
        ck = self.clock[eng]
        wl = [(s, v) for (s, v) in waits.items() if ck.get(s, -1) < v]
        for (s, v) in wl:
            ck[s] = v
        self.sem[sname][1] += amt
        tok = (sname, self.sem[sname][1])
        self.q[eng].append((fn, [(self.sem[s][0], v) for (s, v) in wl],
                            self.sem[sname][0], amt, tag))
        for b in reads:
            self.buf.setdefault(b, {"w": None, "r": []})["r"].append(tok)
        for b in writes:
            self.buf[b] = {"w": tok, "r": []}
        return tok

    def barrier(self):
        tot = {}
        for e in self.buf.values():
            for tok in ([e["w"]] if e["w"] else []) + e["r"]:
                s, v = tok
                tot[s] = max(tot.get(s, 0), v)
        self.barrier_tok = tot
        self.barrier_pending = set(self.ENGINES)

    def final_waits(self, eng):
        out = {}
        for e in self.buf.values():
            for tok in ([e["w"]] if e["w"] else []) + e["r"]:
                s, v = tok
                out[s] = max(out.get(s, 0), v)
        ck = self.clock[eng]
        return [(self.sem[s][0], v) for s, v in out.items()
                if ck.get(s, -1) < v]


def build_program(n_seq=2, l_seq=4096):
    ntile = n_seq * l_seq // TILE
    ngroup = ntile // GTILES
    tps = l_seq // TILE
    nct = tps
    ntab = nct * HALF
    nc = bass.Bass("TRN2", target_bir_lowering=False, debug=False)

    dr = {}
    dr["x_fm"] = nc.dram_tensor("x_fm", [n_seq * D, l_seq], F32,
                                kind="ExternalInput").ap()
    for nm in ("wq_t", "wk_t", "wv_t", "wout_t"):
        dr[nm] = nc.dram_tensor(nm, [D, D], F32, kind="ExternalInput").ap()
    dr["wg_all"] = nc.dram_tensor("wg_all", [D, 24], F32,
                                  kind="ExternalInput").ap()
    dr["b24"] = nc.dram_tensor("b24", [1, 24], F32, kind="ExternalInput").ap()
    dr["bout"] = nc.dram_tensor("bout", [1, D], F32, kind="ExternalInput").ap()
    dr["nw"] = nc.dram_tensor("nw", [1, D], F32, kind="ExternalInput").ap()
    dr["wgate_t"] = nc.dram_tensor("wgate_t", [D, H2], I32,
                                   kind="ExternalInput").ap()
    dr["wup_t"] = nc.dram_tensor("wup_t", [D, H2], I32,
                                 kind="ExternalInput").ap()
    dr["wdown_t"] = nc.dram_tensor("wdown_t", [H2, D], I32,
                                   kind="ExternalInput").ap()
    dr["out_fm"] = nc.dram_tensor("out_fm", [n_seq * D, l_seq], F32,
                                  kind="ExternalOutput").ap()
    dr["wg_c"] = nc.dram_tensor("wg_c", [D, H2], BF16, kind="Internal").ap()
    dr["wu_c"] = nc.dram_tensor("wu_c", [D, H2], BF16, kind="Internal").ap()
    dr["wd_c"] = nc.dram_tensor("wd_c", [H2, D], BF16, kind="Internal").ap()

    PB = [nc.alloc_psum_tensor(f"PB{i}", [128, 512], F32) for i in range(8)]
    BK = [f"@B{i}" for i in range(8)]

    P = _Prog(nc)
    gop = lambda fn, reads=(), writes=(), tag="": P.op(
        "gps", fn, reads, writes, tag=tag)
    gdma = lambda fn, reads=(), writes=(), tag="": P.op(
        "gps", fn, reads, writes, dma=True, tag=tag)
    sdma = lambda fn, reads=(), writes=(), tag="": P.op(
        "sp", fn, reads, writes, dma=True, tag=tag)
    adma = lambda fn, reads=(), writes=(), tag="": P.op(
        "actq", fn, reads, writes, dma=True, tag=tag)
    pe = lambda fn, reads=(), writes=(), tag="": P.op(
        "pe", fn, reads, writes, tag=tag)
    dve = lambda fn, reads=(), writes=(), tag="": P.op(
        "dve", fn, reads, writes, tag=tag)
    act = lambda fn, reads=(), writes=(), tag="": P.op(
        "act", fn, reads, writes, tag=tag)

    sbal = nc.alloc_sbuf_tensor

    cos_bf = sbal("cos_bf", [128, ntab], BF16)
    sin_bf = sbal("sin_bf", [128, ntab], BF16)
    tri_bf = sbal("tri_bf", [128, 128], BF16)
    mneg = sbal("mneg", [128, 128], BF16)
    ident = sbal("ident", [128, 128], BF16)
    ident_f = sbal("ident_f", [1, 1], F32)
    negsel = sbal("negsel", [8, NH * 128], BF16)
    ones_c = sbal("ones_c", [128, 1], BF16)
    ones_r = sbal("ones_r", [1, 128], BF16)
    nones_r = sbal("nones_r", [1, 128], BF16)
    cp1 = sbal("cp1", [128, 1], F32)
    cpeps = sbal("cpeps", [1, 2], F32)
    nw_col = sbal("nw_col", [128, NK], F32)
    bout_c = sbal("bout_c", [128, NK], F32)
    b24_t = sbal("b24_t", [128, 24], F32)
    b24_bf = sbal("b24_bf", [1, 24], BF16)

    gop(lambda g: g.memset(tri_bf[:], 1.0), writes=["tri*"])
    gop(lambda g: g.affine_select(tri_bf[:], tri_bf[:], [[1, 128]],
                                  ALU.is_ge, 0.0, base=0,
                                  channel_multiplier=-1),
        reads=["tri*"], writes=["tri"])
    gop(lambda g: g.memset(mneg[:], 0.0), writes=["mneg*"])
    gop(lambda g: g.affine_select(mneg[:], mneg[:], [[1, 128]],
                                  ALU.is_ge, -1.0e30, base=0,
                                  channel_multiplier=-1),
        reads=["mneg*"], writes=["mneg"])
    gop(lambda g: g.memset(ident[:], 1.0), writes=["ident*"])
    gop(lambda g: g.affine_select(ident[:], ident[:], [[1, 128]],
                                  ALU.is_equal, 0.0, base=0,
                                  channel_multiplier=-1),
        reads=["ident*"], writes=["ident"])
    gop(lambda g: g.memset(ident_f[:], 1.0), writes=["identf"])
    gop(lambda g: g.memset(negsel[:], -1.0), writes=["negsel*"])
    gop(lambda g: g.affine_select(negsel[:], negsel[:], [[1, NH], [0, 128]],
                                  ALU.is_equal, 0.0, base=0,
                                  channel_multiplier=-1),
        reads=["negsel*"], writes=["negsel"])
    gop(lambda g: g.memset(ones_c[:], 1.0), writes=["ones_c"])
    gop(lambda g: g.memset(ones_r[:], 1.0), writes=["ones_r"])
    gop(lambda g: g.memset(nones_r[:], -1.0), writes=["nones_r"])
    gop(lambda g: g.memset(cp1[:], 1.0), writes=["cp1"])
    gop(lambda g: g.memset(cphp[:], math.pi / 2.0), writes=["cphp"])
    gop(lambda g: g.memset(cpeps[:, 0:1], 1.0e-5), writes=["cpeps"])

    gdma(lambda g: g.dma_start(
        nw_col[:], bass.AP(dr["nw"].tensor, 0, [[1, 128], [128, NK]]),
        allow_slow_non_contiguous=True),
        writes=["nw_col"])
    gdma(lambda g: g.dma_start(
        bout_c[:], bass.AP(dr["bout"].tensor, 0, [[1, 128], [128, NK]]),
        allow_slow_non_contiguous=True),
        writes=["bout_c"])
    gdma(lambda g: g.dma_start(b24_bf[:], dr["b24"]), writes=["b24bf"])
    pe(lambda t: t.matmul(PB[6][:, 0:24], ones_r[:], b24_bf[:],
                          start=True, stop=True),
       reads=["ones_r", "b24bf"], writes=[BK[6]], tag="b24bc")
    dve(lambda v: v.tensor_copy(b24_t[:], PB[6][:, 0:24]),
        reads=[BK[6]], writes=["b24"])

    wq = sbal("wq", [128, NK * D], BF16)
    wk = sbal("wk", [128, NK * D], BF16)
    wv = sbal("wv", [128, NK * D], BF16)
    wgs = sbal("wgs", [128, NK * 24], BF16)
    wo = sbal("wo", [128, NK * D], BF16)

    def wload(dst, src_name, width, bufname):
        def fn(g, dst=dst, src_name=src_name, width=width):
            src = bass.AP(dr[src_name].tensor, 0,
                          [[width, 128], [width * 128, NK], [1, width]])
            return g.dma_start(
                dst[:].rearrange("p (k w) -> p k w", k=NK), src)
        gdma(fn, writes=[bufname], tag=f"load:{bufname}")

    wload(wq, "wq_t", D, "wq")
    wload(wk, "wk_t", D, "wk")
    wload(wv, "wv_t", D, "wv")
    wload(wgs, "wg_all", 24, "wgs")
    wload(wo, "wout_t", D, "wo")
    for (cn, sn, rows) in (("wg_c", "wgate_t", D), ("wu_c", "wup_t", D)):
        for hhf in range(2):
            gdma(lambda g, cn=cn, sn=sn, rows=rows, hhf=hhf: g.dma_start(
                dr[cn].rearrange("a b -> a b")[hhf * (rows // 2):
                                               (hhf + 1) * (rows // 2), :],
                dr[sn].rearrange("a b -> a b")[hhf * (rows // 2):
                                               (hhf + 1) * (rows // 2), :]),
                writes=[f"{cn}:{hhf}"], tag=f"cache:{cn}:{hhf}")
    for c2 in range(NC2):
        for p_ in range(2):
            gdma(lambda g, c2=c2, p_=p_: g.dma_start(
                bass.AP(dr["wd_c"].tensor, (c2 * 2 + p_) * 128 * 512,
                        [[512, 128], [1, 512]]),
                bass.AP(dr["wdown_t"].tensor, c2 * 128 * D + p_ * 512,
                        [[D, 128], [1, 512]])),
                writes=[f"wd_c:{c2}:{p_}"], tag=f"cache:wd:{c2}:{p_}")


    from contextlib import ExitStack
    _scratch = ExitStack()

    def scr(name, shape, dtype):
        return _scratch.enter_context(nc.sbuf_tensor(name, shape, dtype))

    tb_f = scr("tb_f", [128, ntab], F32)
    tb_g = scr("tb_g", [128, ntab], F32)
    tb_h = scr("tb_h", [128, ntab], F32)
    tb_i = scr("tb_i", [128, ntab], I32)
    r0f = scr("r0f", [1, ntab], F32)
    r1f = scr("r1f", [1, ntab], F32)
    r0h = scr("r0h", [1, ntab], BF16)
    r0l = scr("r0l", [1, ntab], BF16)
    r1h = scr("r1h", [1, ntab], BF16)
    r1l = scr("r1l", [1, ntab], BF16)
    tvals = scr("tvals", [1, 128], BF16)
    invf = scr("invf", [1, HALF], F32)
    cphp = scr("cphp", [128, 1], F32)
    t_iot = scr("t_iot", [1, 128], F32)

    gop(lambda g: g.iota(t_iot[:], [[1, 128]], base=0, channel_multiplier=0,
                         allow_small_or_imprecise_dtypes=True),
        writes=["iot"])
    act(lambda s: s.activation(invf[:], t_iot[0:1, 0:HALF], AF.Exp,
                               scale=-math.log(THETA) / HALF),
        reads=["iot"], writes=["invf"])
    dve(lambda v: v.tensor_copy(tvals[:], t_iot[:]),
        reads=["iot"], writes=["tvals"])
    gop(lambda g: g.iota(r0f[:], [[1, nct], [0, HALF]], base=0,
                         channel_multiplier=0,
                         allow_small_or_imprecise_dtypes=True),
        writes=["r0f"])

    def _rhs0(v):
        invb = bass.AP(invf, 0, [[HALF, 1], [0, nct], [1, HALF]])
        return v.tensor_tensor(
            r0f[:].rearrange("o (c i) -> o c i", i=HALF),
            r0f[:].rearrange("o (c i) -> o c i", i=HALF),
            invb, ALU.mult)
    dve(_rhs0, reads=["r0f", "invf"], writes=["r0f"])
    dve(lambda v: v.tensor_scalar_mul(r0f[:], r0f[:], 128.0),
        reads=["r0f"], writes=["r0f"])

    def _rhs1(v):
        invb = bass.AP(invf, 0, [[HALF, 1], [0, nct], [1, HALF]])
        return v.tensor_copy(r1f[:].rearrange("o (c i) -> o c i", i=HALF),
                             invb)
    dve(_rhs1, reads=["invf"], writes=["r1f"])
    dve(lambda v: v.tensor_copy(r0h[:], r0f[:]), reads=["r0f"],
        writes=["r0h"])
    dve(lambda v: v.tensor_tensor(r0l[:], r0f[:], r0h[:], ALU.subtract),
        reads=["r0f", "r0h"], writes=["r0l"])
    dve(lambda v: v.tensor_copy(r1h[:], r1f[:]), reads=["r1f"],
        writes=["r1h"])
    dve(lambda v: v.tensor_tensor(r1l[:], r1f[:], r1h[:], ALU.subtract),
        reads=["r1f", "r1h"], writes=["r1l"])
    nq = min(ntab, 512)
    nmm = (ntab + nq - 1) // nq
    for i in range(nmm):
        bk = i % 4
        sl_ = slice(i * nq, (i + 1) * nq)

        def fmm(t, sl_=sl_, bk=bk):
            t.matmul(PB[bk][:, 0:nq], ones_r[:], r0h[0:1, sl_],
                     start=True, stop=False)
            t.matmul(PB[bk][:, 0:nq], ones_r[:], r0l[0:1, sl_],
                     start=False, stop=False)
            t.matmul(PB[bk][:, 0:nq], tvals[:], r1h[0:1, sl_],
                     start=False, stop=False)
            return t.matmul(PB[bk][:, 0:nq], tvals[:], r1l[0:1, sl_],
                            start=False, stop=True)
        pe(fmm, reads=["ones_r", "tvals", "r0h", "r0l", "r1h", "r1l"],
           writes=[BK[bk]], tag="freqmm")
        dve(lambda v, sl_=sl_, bk=bk: v.tensor_copy(tb_f[:, sl_],
                                                    PB[bk][:, 0:nq]),
            reads=[BK[bk]], writes=["tb_f"])
    dve(lambda v: v.tensor_scalar(tb_g[:], tb_f[:], 1.0 / TWO_PI, 0.5,
                                  ALU.mult, ALU.add),
        reads=["tb_f"], writes=["tb_g"])
    dve(lambda v: v.tensor_copy(tb_i[:], tb_g[:]),
        reads=["tb_g"], writes=["tb_i"])
    dve(lambda v: v.tensor_copy(tb_g[:], tb_i[:]),
        reads=["tb_i"], writes=["tb_g"])
    for c in (C1, C2, C3):
        dve(lambda v, c=c: v.scalar_tensor_tensor(
            tb_f[:], tb_g[:], -c, tb_f[:], ALU.mult, ALU.add),
            reads=["tb_f", "tb_g"], writes=["tb_f"])
    dve(lambda v: v.tensor_scalar_min(tb_f[:], tb_f[:], math.pi),
        reads=["tb_f"], writes=["tb_f"])
    dve(lambda v: v.tensor_scalar_max(tb_f[:], tb_f[:], -math.pi),
        reads=["tb_f"], writes=["tb_f"])
    act(lambda s: s.activation(tb_h[:], tb_f[:], AF.Sin),
        reads=["tb_f"], writes=["tb_h"])
    dve(lambda v: v.tensor_copy(sin_bf[:], tb_h[:]),
        reads=["tb_h"], writes=["sin"])
    act(lambda s: s.activation(tb_g[:], tb_f[:], AF.Abs),
        reads=["tb_f"], writes=["tb_g"])
    act(lambda s: s.activation(tb_h[:], tb_g[:], AF.Sin, scale=-1.0,
                               bias=cphp[:, 0:1]),
        reads=["tb_g", "cphp"], writes=["tb_h"])
    dve(lambda v: v.tensor_copy(cos_bf[:], tb_h[:]),
        reads=["tb_h"], writes=["cos"])

    _scratch.close()

    wg_win = sbal("wg_win", [128, 2 * D], BF16)
    wu_win = sbal("wu_win", [128, 2 * D], BF16)
    wd_win = sbal("wd_win", [128, 4 * 512], BF16)
    xn_fm = sbal("xn_fm", [128, NK * GTOK], BF16)
    partial = sbal("partial", [128, NK * GTOK], BF16)
    hh = sbal("hh", [128, NC2 * GTOK], BF16)
    xf_st = sbal("xf_st", [128, 2 * D], BF16)
    out_sb = sbal("out_sb", [128, 2 * 256], F32)
    q_tm = sbal("q_tm", [128, D], BF16)
    k_tm = sbal("k_tm", [128, D], BF16)
    v_tm = sbal("v_tm", [128, D], BF16)
    kw_tm = sbal("kw_tm", [128, D], BF16)
    xnw_st = sbal("xnw_st", [128, NK * 128], BF16)
    kfm = sbal("kfm", [128, 8 * 128], BF16)
    qofm = sbal("qofm", [128, 8 * 128], BF16)
    qhfm = sbal("qhfm", [128, 4 * 128], BF16)
    shat = sbal("shat", [128, 4 * 128], BF16)
    atile = sbal("atile", [128, 4 * 128], BF16)
    dtile = sbal("dtile", [128, 4 * 128], BF16)
    mbf = sbal("mbf", [128, 4 * 128], BF16)
    yfm = sbal("yfm", [128, NH * 128], BF16)
    msb = sbal("msb", [128, n_seq * NH * 128], F32)
    zrow = sbal("zrow", [128, 24], F32)
    gsc = sbal("gsc", [128, 24], F32)
    spb = sbal("spb", [128, 16], F32)
    sp_hi = sbal("sp_hi", [128, 16], BF16)
    sp_lo = sbal("sp_lo", [128, 16], BF16)
    p_tm = sbal("p_tm", [128, 8], F32)
    pg_hi = sbal("pg_hi", [8, 128], BF16)
    pg_lo = sbal("pg_lo", [8, 128], BF16)
    pc_hi = sbal("pc_hi", [1, 8], BF16)
    pc_lo = sbal("pc_lo", [1, 8], BF16)
    b2p = sbal("b2p", [128, 8], F32)
    w_tm = sbal("w_tm", [128, 8], BF16)
    dc_t = sbal("dc_t", [128, 8], F32)
    wargt = sbal("wargt", [128, 8], F32)
    srow = sbal("srow", [1, 4 * 128], F32)
    srow_bf = sbal("srow_bf", [1, 2 * 128], BF16)
    acol = sbal("acol", [128, 1], F32)
    s_sb = sbal("s_sb", [128, 128], BF16)

    gop(lambda g: g.memset(msb[:], 0.0),
        writes=[f"M:{i}" for i in range(n_seq * NH)])

    P.barrier()

    def x_load(tile):
        par = tile % 2
        seq = tile // tps
        c = tile % tps

        def fn(g, par=par, seq=seq, c=c):
            src = bass.AP(dr["x_fm"].tensor, seq * D * l_seq + c * TILE,
                          [[l_seq, 128], [l_seq * 128, NK], [1, TILE]])
            return g.dma_start(
                xf_st[:, par * D:(par + 1) * D].rearrange(
                    "p (k t) -> p k t", k=NK), src)
        gdma(fn, writes=[f"xf:{par}"], tag=f"xload:{tile}")

    def R(i):
        return srow[0:1, i * 128:(i + 1) * 128]

    def ln_thunks(tile):
        TH = []

        def dpe(*a, **kw):
            TH.append(lambda a=a, kw=kw: pe(*a, **kw))

        def ddve(*a, **kw):
            TH.append(lambda a=a, kw=kw: dve(*a, **kw))

        def dact(*a, **kw):
            TH.append(lambda a=a, kw=kw: act(*a, **kw))
        _ln_body(tile, dpe, ddve, dact, square=False)
        return TH

    def ln_square(tile):
        par = tile % 2
        ti = tile % GTILES
        xf = xf_st[:, par * D:(par + 1) * D]
        toff0 = ti * TILE
        sq_names = [f"pt:{k}:{ti}" for k in range(NK)]
        sqfull = bass.AP(partial.tensor if hasattr(partial, "tensor")
                         else partial, toff0,
                         [[NK * GTOK, 128], [GTOK, NK], [1, TILE]])
        dve(lambda v: v.tensor_tensor(sqfull, xf[:].rearrange(
            "p (k t) -> p k t", k=NK), xf[:].rearrange(
            "p (k t) -> p k t", k=NK), ALU.mult),
            reads=[f"xf:{par}"], writes=sq_names)

    def ln_stage(tile):
        ln_square(tile)
        for th in ln_thunks(tile):
            th()

    def _ln_body(tile, pe, dve, act, square=True):
        par = tile % 2
        ti = tile % GTILES
        xf = xf_st[:, par * D:(par + 1) * D]

        def sx(t):
            last = None
            for k in range(NK):
                last = t.matmul(PB[0][0:1, 0:TILE], ones_c[:],
                                xf[:, k * TILE:(k + 1) * TILE],
                                start=(k == 0), stop=(k == NK - 1))
            return last
        pe(sx, reads=[f"xf:{par}", "ones_c"], writes=[BK[0]], tag="sxmm")
        toff0 = ti * TILE
        sq_names = [f"pt:{k}:{ti}" for k in range(NK)]
        if square:
            sqfull = bass.AP(partial.tensor if hasattr(partial, "tensor")
                             else partial, toff0,
                             [[NK * GTOK, 128], [GTOK, NK], [1, TILE]])
            dve(lambda v: v.tensor_tensor(sqfull, xf[:].rearrange(
                "p (k t) -> p k t", k=NK), xf[:].rearrange(
                "p (k t) -> p k t", k=NK), ALU.mult),
                reads=[f"xf:{par}"], writes=sq_names)
        for k in range(NK):
            sqap = partial[:, k * GTOK + toff0:k * GTOK + toff0 + TILE]
            pe(lambda t, k=k, sqap=sqap: t.matmul(
                PB[1][0:1, 0:TILE], ones_c[:], sqap,
                start=(k == 0), stop=(k == NK - 1)),
                reads=[f"pt:{k}:{ti}", "ones_c"], writes=[BK[1]], tag="sxxmm")
        dve(lambda v: v.tensor_scalar_mul(R(0), PB[0][0:1, 0:TILE], 1.0 / D),
            reads=[BK[0]], writes=["srow"])
        dve(lambda v: v.tensor_copy(srow_bf[0:1, 0:128], R(0)),
            reads=["srow"], writes=["srbf"])
        dve(lambda v: v.tensor_tensor(R(1), R(0), R(0), ALU.mult),
            reads=["srow"], writes=["srow"])
        dve(lambda v: v.scalar_tensor_tensor(R(1), PB[1][0:1, 0:TILE],
                                             1.0 / D, R(1),
                                             ALU.mult, ALU.subtract),
            reads=[BK[1], "srow"], writes=["srow"])
        act(lambda s: s.activation(R(2), R(1), AF.Ln, bias=cpeps[0:1, 0:1]),
            reads=["srow", "cpeps"], writes=["srow"])
        act(lambda s: s.activation(R(3), R(2), AF.Exp, scale=-0.5),
            reads=["srow"], writes=["srow"])
        dve(lambda v: v.tensor_copy(srow_bf[0:1, 128:256], R(3)),
            reads=["srow"], writes=["srbf"])
        act(lambda s: s.activation(R(0), R(2), AF.Exp, scale=-1.0),
            reads=["srow"], writes=["srow"])
        dve(lambda v: v.tensor_tensor(R(3), R(1), R(0), ALU.mult),
            reads=["srow"], writes=["srow"])
        dve(lambda v: v.tensor_scalar_add(R(3), R(3), 1.0e-6),
            reads=["srow"], writes=["srow"])
        act(lambda s: s.activation(R(2), R(3), AF.Ln),
            reads=["srow"], writes=["srow"])
        act(lambda s: s.activation(R(0), R(2), AF.Exp, scale=-0.5),
            reads=["srow"], writes=["srow"])
        pe(lambda t: t.transpose(PB[0][:, 128:129],
                                 R(0), ident_f[0:1, 0:1]),
           reads=["srow", "identf"], writes=[BK[0]], tag="alphaT")
        dve(lambda v: v.tensor_copy(acol[:], PB[0][:, 128:129]),
            reads=[BK[0]], writes=["acol"])
        pe(lambda t: t.matmul(PB[0][:, 256:384], ones_r[:],
                              srow_bf[0:1, 128:256], start=True, stop=True),
           reads=["ones_r", "srbf"], writes=[BK[0]], tag="sbc")
        act(lambda s: s.activation(s_sb[:], PB[0][:, 256:384], AF.Copy),
            reads=[BK[0]], writes=["s_sb"])
        toff = ti * TILE
        for k in range(NK):
            bk = k % 2
            pe(lambda t, k=k, bk=bk: (
                t.matmul(PB[bk][:, 384:512], ident[:],
                         xf[:, k * TILE:(k + 1) * TILE],
                         start=True, stop=False),
                t.matmul(PB[bk][:, 384:512], nones_r[:],
                         srow_bf[0:1, 0:128], start=False, stop=True))[-1],
                reads=["ident", f"xf:{par}", "nones_r", "srbf"],
                writes=[BK[bk]], tag="psumx")
            dve(lambda v, k=k, bk=bk, toff=toff: v.tensor_tensor(
                xn_fm[:, k * GTOK + toff:k * GTOK + toff + TILE],
                PB[bk][:, 384:512], s_sb[:], ALU.mult),
                reads=[BK[bk], "s_sb"], writes=[f"xn:{k}:{ti}"], tag="xn")
            dve(lambda v, k=k, toff=toff: v.tensor_scalar(
                xnw_st[:, k * 128:(k + 1) * 128],
                xn_fm[:, k * GTOK + toff:k * GTOK + toff + TILE],
                nw_col[:, k:k + 1], None, ALU.mult),
                reads=[f"xn:{k}:{ti}", "nw_col"], writes=[f"xnw:{k}"],
                tag="xnw")

    def pass_a_qz(tile):
        for k in range(NK):
            xk = xnw_st[:, k * 128:(k + 1) * 128]
            rd = [f"xnw:{k}", "wq", "wgs"]

            def fq(t, k=k, xk=xk):
                t.matmul(PB[0][:, 0:512], xk, wq[:, k * D:k * D + 512],
                         start=(k == 0), stop=(k == NK - 1))
                return t.matmul(PB[1][:, 0:512], xk,
                                wq[:, k * D + 512:(k + 1) * D],
                                start=(k == 0), stop=(k == NK - 1))
            pe(fq, reads=rd, writes=[BK[0], BK[1]], tag="mmQ")
            pe(lambda t, k=k, xk=xk: t.matmul(
                PB[6][:, 0:24], xk, wgs[:, k * 24:(k + 1) * 24],
                start=(k == 0), stop=(k == NK - 1)),
                reads=rd, writes=[BK[6]], tag="mmZ")

    def pass_a_k(tile):
        for k in range(NK):
            xk = xnw_st[:, k * 128:(k + 1) * 128]
            rd = [f"xnw:{k}", "wk"]

            def fk(t, k=k, xk=xk):
                t.matmul(PB[2][:, 0:512], xk, wk[:, k * D:k * D + 512],
                         start=(k == 0), stop=(k == NK - 1))
                return t.matmul(PB[3][:, 0:512], xk,
                                wk[:, k * D + 512:(k + 1) * D],
                                start=(k == 0), stop=(k == NK - 1))
            pe(fk, reads=rd, writes=[BK[2], BK[3]], tag="mmK")

    def pass_a_v(tile):
        for k in range(NK):
            xk = xnw_st[:, k * 128:(k + 1) * 128]
            rd = [f"xnw:{k}", "wv"]

            def fv(t, k=k, xk=xk):
                t.matmul(PB[4][:, 0:512], xk, wv[:, k * D:k * D + 512],
                         start=(k == 0), stop=(k == NK - 1))
                return t.matmul(PB[5][:, 0:512], xk,
                                wv[:, k * D + 512:(k + 1) * D],
                                start=(k == 0), stop=(k == NK - 1))
            pe(fv, reads=rd, writes=[BK[4], BK[5]], tag="mmV")

    def gates_pre(tile):
        dve(lambda v: v.tensor_scalar(zrow[:], PB[6][:, 0:24], acol[:, 0:1],
                                      None, ALU.mult),
            reads=[BK[6], "acol"], writes=["zrow"])
        dve(lambda v: v.tensor_tensor(zrow[:], zrow[:], b24_t[:], ALU.add),
            reads=["zrow", "b24"], writes=["zrow"])
        act(lambda s: s.activation(gsc[:], zrow[:], AF.Exp, scale=-1.0),
            reads=["zrow"], writes=["gscA"])
        act(lambda s: s.activation(gsc[:], gsc[:], AF.Ln, bias=cp1[:, 0:1]),
            reads=["gscA"], writes=["gscA"])
        dve(lambda v: v.tensor_scalar_min(spb[:, 0:8], gsc[:, 0:8], 80.0),
            reads=["gscA"], writes=["spb"])
        dve(lambda v: v.tensor_copy(spb[:, 8:16], gsc[:, 16:24]),
            reads=["gscA"], writes=["spb"])
        dve(lambda v: v.tensor_copy(sp_hi[:], spb[:]),
            reads=["spb"], writes=["sphi"])
        dve(lambda v: v.tensor_tensor(sp_lo[:], spb[:], sp_hi[:],
                                      ALU.subtract),
            reads=["spb", "sphi"], writes=["splo"])

    def gates_mms_a(tile):
        pe(lambda t: (
            t.matmul(PB[6][:, 32:40], tri_bf[:], sp_hi[:, 0:8],
                     start=True, stop=False),
            t.matmul(PB[6][:, 32:40], tri_bf[:], sp_lo[:, 0:8],
                     start=False, stop=True))[-1],
            reads=["tri", "sphi", "splo"], writes=[BK[6]], tag="Pmm")
        dve(lambda v: v.tensor_copy(p_tm[:], PB[6][:, 32:40]),
            reads=[BK[6]], writes=["p_tm"])
        pe(lambda t: (
            t.matmul(PB[7][0:8, 0:128], sp_hi[:, 0:8], tri_bf[:],
                     start=True, stop=False),
            t.matmul(PB[7][0:8, 0:128], sp_lo[:, 0:8], tri_bf[:],
                     start=False, stop=False),
            t.matmul(PB[7][0:8, 0:128], sp_hi[:, 8:16], ident[:],
                     start=False, stop=False),
            t.matmul(PB[7][0:8, 0:128], sp_lo[:, 8:16], ident[:],
                     start=False, stop=True))[-1],
            reads=["tri", "ident", "sphi", "splo"], writes=[BK[7]],
            tag="Pgmm")
        dve(lambda v: v.tensor_copy(pg_hi[:], PB[7][0:8, 0:128]),
            reads=[BK[7]], writes=["pghi"])
        dve(lambda v: v.tensor_tensor(pg_lo[:], PB[7][0:8, 0:128], pg_hi[:],
                                      ALU.subtract),
            reads=[BK[7], "pghi"], writes=["pglo"])
        pe(lambda t: (
            t.matmul(PB[6][0:1, 48:56], ones_c[:], sp_hi[:, 0:8],
                     start=True, stop=False),
            t.matmul(PB[6][0:1, 48:56], ones_c[:], sp_lo[:, 0:8],
                     start=False, stop=True))[-1],
            reads=["ones_c", "sphi", "splo"], writes=[BK[6]], tag="PCmm")
        dve(lambda v: v.tensor_copy(pc_hi[:], PB[6][0:1, 48:56]),
            reads=[BK[6]], writes=["pchi"])
        dve(lambda v: v.tensor_tensor(pc_lo[:], PB[6][0:1, 48:56], pc_hi[:],
                                      ALU.subtract),
            reads=[BK[6], "pchi"], writes=["pclo"])
        dve(lambda v: v.tensor_tensor(b2p[:], p_tm[:], gsc[:, 8:16],
                                      ALU.subtract),
            reads=["p_tm", "gscA"], writes=["b2p"])
        dve(lambda v: v.tensor_scalar_add(b2p[:], b2p[:], LNSCALE),
            reads=["b2p"], writes=["b2p"])

    def gates_mms_b(tile):
        pe(lambda t: (
            t.matmul(PB[6][:, 64:72], ones_r[:], pc_hi[:],
                     start=True, stop=False),
            t.matmul(PB[6][:, 64:72], ones_r[:], pc_lo[:],
                     start=False, stop=True))[-1],
            reads=["ones_r", "pchi", "pclo"], writes=[BK[6]], tag="PCbmm")
        act(lambda s: s.activation(dc_t[:], PB[6][:, 64:72], AF.Exp,
                                   scale=-1.0),
            reads=[BK[6]], writes=["dc"])
        dve(lambda v: v.tensor_tensor(wargt[:], b2p[:], PB[6][:, 64:72],
                                      ALU.subtract),
            reads=["b2p", BK[6]], writes=["wargt"])
        act(lambda s: s.activation(w_tm[:], wargt[:], AF.Exp),
            reads=["wargt"], writes=["w_tm"])

    def evac_q(tile):
        act(lambda s: s.activation(q_tm[:, 0:512], PB[0][:, 0:512], AF.Copy,
                                   scale=acol[:, 0:1]),
            reads=[BK[0], "acol"], writes=["q_tm"])
        act(lambda s: s.activation(q_tm[:, 512:1024], PB[1][:, 0:512],
                                   AF.Copy, scale=acol[:, 0:1]),
            reads=[BK[1], "acol"], writes=["q_tm"])

    def evac_k(tile):
        act(lambda s: s.activation(k_tm[:, 0:512], PB[2][:, 0:512], AF.Copy,
                                   scale=acol[:, 0:1]),
            reads=[BK[2], "acol"], writes=["k_tm"])
        act(lambda s: s.activation(k_tm[:, 512:1024], PB[3][:, 0:512],
                                   AF.Copy, scale=acol[:, 0:1]),
            reads=[BK[3], "acol"], writes=["k_tm"])

    def evac_v(tile):
        dve(lambda v: v.tensor_scalar(v_tm[:, 0:512],
                                      PB[4][:, 0:512], acol[:, 0:1], None,
                                      ALU.mult),
            reads=[BK[4], "acol"], writes=["v_tm"])
        dve(lambda v: v.tensor_scalar(v_tm[:, 512:D],
                                      PB[5][:, 0:512], acol[:, 0:1], None,
                                      ALU.mult),
            reads=[BK[5], "acol"], writes=["v_tm"])

    def _rope_one(tile, src_sb, srcb):
        c = tile % tps
        ti = tile % GTILES
        toff = ti * TILE
        cosap = bass.AP(cos_bf, c * HALF,
                        [[ntab, 128], [0, 4], [0, 2], [1, HALF]])
        sinap = bass.AP(sin_bf, c * HALF,
                        [[ntab, 128], [0, 4], [0, 2], [1, HALF]])
        lo = bass.AP(src_sb, 0, [[D, 128], [256, 4], [128, 2], [1, HALF]])
        hi = bass.AP(src_sb, HALF, [[D, 128], [256, 4], [128, 2], [1, HALF]])
        rsa = bass.AP(partial, toff, [[NK * GTOK, 128], [GTOK, 4],
                                      [HALF, 2], [1, HALF]])
        rsb = bass.AP(partial, 4 * GTOK + toff,
                      [[NK * GTOK, 128], [GTOK, 4], [HALF, 2], [1, HALF]])
        rsc = bass.AP(kw_tm, 0, [[D, 128], [128, 8], [1, HALF]])
        rsd = bass.AP(kw_tm, HALF, [[D, 128], [128, 8], [1, HALF]])
        lo3 = bass.AP(src_sb, 0, [[D, 128], [DH, NH], [1, HALF]])
        hi3 = bass.AP(src_sb, HALF, [[D, 128], [DH, NH], [1, HALF]])
        cos3 = bass.AP(cos_bf, c * HALF, [[ntab, 128], [0, NH], [1, HALF]])
        sin3 = bass.AP(sin_bf, c * HALF, [[ntab, 128], [0, NH], [1, HALF]])
        pt_lo = [f"pt:{k}:{ti}" for k in range(4)]
        pt_hi = [f"pt:{k}:{ti}" for k in range(4, NK)]
        dve(lambda v: v.tensor_tensor(rsa, lo, cosap, ALU.mult),
            reads=[srcb, "cos"], writes=pt_lo)
        dve(lambda v: v.tensor_tensor(rsb, hi, sinap, ALU.mult),
            reads=[srcb, "sin"], writes=pt_hi)
        dve(lambda v: v.tensor_tensor(rsc, lo3, sin3, ALU.mult),
            reads=[srcb, "sin"], writes=["kw"])
        dve(lambda v: v.tensor_tensor(rsd, hi3, cos3, ALU.mult),
            reads=[srcb, "cos"], writes=["kw"])
        dve(lambda v: v.tensor_tensor(lo, rsa, rsb, ALU.subtract),
            reads=pt_lo + pt_hi, writes=[srcb])
        dve(lambda v: v.tensor_tensor(hi3, rsc, rsd, ALU.add),
            reads=["kw"], writes=[srcb])

    def rope_q(tile):
        _rope_one(tile, q_tm, "q_tm")

    def rope_k_core(tile):
        _rope_one(tile, k_tm, "k_tm")

    def issue_tq(tile):
        for h in range(NH):
            r6 = h
            qd = sdma if h % 2 == 0 else adma
            qd(lambda e, h=h, r6=r6: e.dma_start(
                qofm[:, r6 * 128:(r6 + 1) * 128],
                q_tm[:, h * DH:(h + 1) * DH], transpose=True),
                reads=["q_tm"], writes=[f"qofm:{r6}"], tag=f"Tq:{h}")

    def issue_tk(tile):
        for h in range(NH):
            r6 = h
            qd = sdma if h % 2 == 1 else adma
            qd(lambda e, h=h, r6=r6: e.dma_start(
                kfm[:, r6 * 128:(r6 + 1) * 128],
                k_tm[:, h * DH:(h + 1) * DH], transpose=True),
                reads=["k_tm"], writes=[f"kfm:{r6}"], tag=f"Tk:{h}")

    def kw_fold(tile):
        wb = bass.AP(w_tm, 0, [[8, 128], [1, NH], [0, DH]])
        dve(lambda v: v.tensor_tensor(
            kw_tm[:].rearrange("p (h e) -> p h e", e=DH),
            k_tm[:].rearrange("p (h e) -> p h e", e=DH), wb, ALU.mult),
            reads=["k_tm", "w_tm"], writes=["kw"])

    def _sb_slots(h):
        r = h % 4
        r6 = h % 6
        return r, r6, BK[2 + r6], PB[2 + r6]

    def scan_mm(tile, h):
        r, r6, bs, pbs = _sb_slots(h)
        pe(lambda t: (
            t.matmul(pbs[:, 0:128], negsel[:, h * 128:(h + 1) * 128],
                     pg_hi[:], start=True, stop=False),
            t.matmul(pbs[:, 0:128], negsel[:, h * 128:(h + 1) * 128],
                     pg_lo[:], start=False, stop=True))[-1],
            reads=["negsel", "pghi", "pglo"], writes=[bs], tag="Lbc")

    def scan_s(tile, h):
        r, r6, bs, pbs = _sb_slots(h)
        pe(lambda t: t.matmul(pbs[:, 128:256], kfm[:, h * 128:(h + 1) * 128],
                              qofm[:, h * 128:(h + 1) * 128],
                              start=True, stop=True),
           reads=[f"kfm:{h}", f"qofm:{h}"], writes=[bs], tag="S")

    def scan_mneg(tile, h):
        r, r6, bs, pbs = _sb_slots(h)
        act(lambda s: s.activation(dtile[:, r * 128:(r + 1) * 128],
                                   pbs[:, 0:128], AF.Exp),
            reads=[bs], writes=[f"dtile:{r}"])
        dve(lambda v: v.tensor_tensor(pbs[:, 0:128], pbs[:, 0:128],
                                      mneg[:], ALU.add),
            reads=[bs, "mneg"], writes=[bs])

    def scan_exp(tile, h):
        r, r6, bs, pbs = _sb_slots(h)
        seq = tile // tps
        mslot = seq * NH + h
        act(lambda s: s.activation(atile[:, r * 128:(r + 1) * 128],
                                   pbs[:, 0:128], AF.Exp,
                                   bias=b2p[:, h:h + 1]),
            reads=[bs, "b2p"], writes=[f"atile:{r}"])
        gop(lambda g: g.tensor_copy(mbf[:, r * 128:(r + 1) * 128],
                                    msb[:, mslot * 128:(mslot + 1) * 128]),
            reads=[f"M:{mslot}"], writes=[f"mbf:{r}"])

    def scan_shat(tile, h):
        r, r6, bs, pbs = _sb_slots(h)
        dve(lambda v: v.tensor_tensor(shat[:, r * 128:(r + 1) * 128],
                                      pbs[:, 128:256],
                                      atile[:, r * 128:(r + 1) * 128],
                                      ALU.mult),
            reads=[bs, f"atile:{r}"], writes=[f"shat:{r}"])
        dve(lambda v: v.tensor_tensor(qhfm[:, r * 128:(r + 1) * 128],
                                      qofm[:, h * 128:(h + 1) * 128],
                                      dtile[:, r * 128:(r + 1) * 128],
                                      ALU.mult),
            reads=[f"qofm:{h}", f"dtile:{r}"], writes=[f"qhfm:{r}"])

    def scan_y(tile, h):
        r, r6, bs, pbs = _sb_slots(h)
        pe(lambda t: (
            t.matmul(pbs[:, 256:384],
                     v_tm[:, h * DH:(h + 1) * DH],
                     shat[:, r * 128:(r + 1) * 128], start=True, stop=False),
            t.matmul(pbs[:, 256:384], mbf[:, r * 128:(r + 1) * 128],
                     qhfm[:, r * 128:(r + 1) * 128],
                     start=False, stop=True))[-1],
            reads=["v_tm", f"shat:{r}", f"mbf:{r}", f"qhfm:{r}"],
            writes=[bs], tag="Y")
        act(lambda s: s.activation(yfm[:, h * 128:(h + 1) * 128],
                                   pbs[:, 256:384], AF.Copy),
            reads=[bs], writes=[f"y:{h}"])

    def scan_kv(tile, h):
        r, r6, bs, pbs = _sb_slots(h)
        seq = tile // tps
        mslot = seq * NH + h
        msl = msb[:, mslot * 128:(mslot + 1) * 128]
        pe(lambda t: t.matmul(pbs[:, 384:512], kw_tm[:, h * DH:(h + 1) * DH],
                              v_tm[:, h * DH:(h + 1) * DH],
                              start=True, stop=True),
           reads=["kw", "v_tm"], writes=[bs], tag="KV")
        dve(lambda v: v.scalar_tensor_tensor(
            msl, msl, dc_t[:, h:h + 1], pbs[:, 384:512], ALU.mult, ALU.add),
            reads=[f"M:{mslot}", "dc", bs], writes=[f"M:{mslot}"], tag="Mupd")

    def outproj(tile):
        par = tile % 2
        ti = tile % GTILES
        yreads = [f"y:{h}" for h in range(NH)]
        toff = ti * TILE
        for ec in range(NK):
            bk = 2 + ec % 4
            off = 128 * (ec // 4)

            def fn(t, ec=ec, bk=bk, off=off):
                last = None
                for h in range(NH):
                    last = t.matmul(
                        PB[bk][:, off:off + 128],
                        wo[:, h * D + ec * 128:h * D + (ec + 1) * 128],
                        yfm[:, h * 128:(h + 1) * 128],
                        start=(h == 0), stop=(h == NH - 1))
                return last
            pe(fn, reads=["wo"] + yreads, writes=[BK[bk]], tag="outproj")
            dve(lambda v, ec=ec, bk=bk, off=off, toff=toff:
                v.scalar_tensor_tensor(
                    partial[:, ec * GTOK + toff:ec * GTOK + toff + TILE],
                    PB[bk][:, off:off + 128], bout_c[:, ec:ec + 1],
                    xf_st[:, par * D + ec * 128:par * D + (ec + 1) * 128],
                    ALU.add, ALU.add),
                reads=[BK[bk], "bout_c", f"xf:{par}"],
                writes=[f"pt:{ec}:{ti}"], tag="partial")

    def ffn_group(gi):
        xn_reads = [f"xn:{k}:{tt}" for k in range(NK) for tt in range(GTILES)]
        for c2 in range(NC2):
            wpar = c2 % 2
            gdma(lambda g, c2=c2, wpar=wpar: g.dma_start(
                wg_win[:, wpar * D:(wpar + 1) * D].rearrange(
                    "p (k w) -> p k w", k=NK),
                bass.AP(dr["wg_c"].tensor, c2 * 128,
                        [[H2, 128], [H2 * 128, NK], [1, 128]])),
                reads=["wg_c:0", "wg_c:1"],
                writes=[f"wgw:{wpar}"], tag=f"wg:{c2}")
            gdma(lambda g, c2=c2, wpar=wpar: g.dma_start(
                wu_win[:, wpar * D:(wpar + 1) * D].rearrange(
                    "p (k w) -> p k w", k=NK),
                bass.AP(dr["wu_c"].tensor, c2 * 128,
                        [[H2, 128], [H2 * 128, NK], [1, 128]])),
                reads=["wu_c:0", "wu_c:1"],
                writes=[f"wuw:{wpar}"], tag=f"wu:{c2}")
            gb = (0, 1) if c2 % 2 == 0 else (4, 5)
            ub = (2, 3) if c2 % 2 == 0 else (6, 7)

            def fgate(t, wpar=wpar, gb=gb):
                last = None
                for k in range(NK):
                    w = wg_win[:, wpar * D + k * 128:wpar * D + (k + 1) * 128]
                    last = t.matmul(PB[gb[0]][:, 0:512], w,
                                    xn_fm[:, k * GTOK:k * GTOK + 512],
                                    start=(k == 0), stop=(k == NK - 1))
                    last = t.matmul(PB[gb[1]][:, 0:512], w,
                                    xn_fm[:, k * GTOK + 512:(k + 1) * GTOK],
                                    start=(k == 0), stop=(k == NK - 1))
                return last
            pe(fgate, reads=[f"wgw:{wpar}"] + xn_reads,
               writes=[BK[gb[0]], BK[gb[1]]], tag="ffn_gate")

            def fup(t, wpar=wpar, ub=ub):
                last = None
                for k in range(NK):
                    w = wu_win[:, wpar * D + k * 128:wpar * D + (k + 1) * 128]
                    last = t.matmul(PB[ub[0]][:, 0:512], w,
                                    xn_fm[:, k * GTOK:k * GTOK + 512],
                                    start=(k == 0), stop=(k == NK - 1))
                    last = t.matmul(PB[ub[1]][:, 0:512], w,
                                    xn_fm[:, k * GTOK + 512:(k + 1) * GTOK],
                                    start=(k == 0), stop=(k == NK - 1))
                return last
            pe(fup, reads=[f"wuw:{wpar}"] + xn_reads,
               writes=[BK[ub[0]], BK[ub[1]]], tag="ffn_up")
            for half in range(2):
                act(lambda s, c2=c2, gb=gb, half=half: s.activation(
                    hh[:, c2 * GTOK + half * 512:c2 * GTOK + (half + 1) * 512],
                    PB[gb[half]][:, 0:512], AF.Silu, scale=1.0 / 127.0),
                    reads=[BK[gb[half]]], writes=[f"hh:{c2}"], tag="silu")
                dve(lambda v, c2=c2, ub=ub, half=half: v.scalar_tensor_tensor(
                    hh[:, c2 * GTOK + half * 512:c2 * GTOK + (half + 1) * 512],
                    PB[ub[half]][:, 0:512], 1.0 / 127.0,
                    hh[:, c2 * GTOK + half * 512:c2 * GTOK + (half + 1) * 512],
                    ALU.mult, ALU.mult),
                    reads=[BK[ub[half]], f"hh:{c2}"], writes=[f"hh:{c2}"],
                    tag="hhmul")
        seq = (gi * GTILES) // tps
        goff = ((gi * GTILES) % tps) * TILE
        for p in range(2):
            AH = 3

            def _wd_dma(c2, p=p):
                wpar = c2 % 4
                gdma(lambda g, c2=c2, p=p, wpar=wpar: g.dma_start(
                    wd_win[:, wpar * 512:(wpar + 1) * 512],
                    bass.AP(dr["wd_c"].tensor, (c2 * 2 + p) * 128 * 512,
                            [[512, 128], [1, 512]])),
                    reads=[f"wd_c:{c2}:{p}"],
                    writes=[f"wdw:{wpar}"], tag=f"wd:{p}:{c2}")

            def _wd_mm(c2, p=p):
                wpar = c2 % 4

                def fdown(t, c2=c2, wpar=wpar):
                    last = None
                    for dc in range(4):
                        w = wd_win[:, wpar * 512 + dc * 128:
                                   wpar * 512 + (dc + 1) * 128]
                        last = t.matmul(PB[2 * dc][:, 0:512], w,
                                        hh[:, c2 * GTOK:c2 * GTOK + 512],
                                        start=(c2 == 0), stop=(c2 == NC2 - 1))
                        last = t.matmul(PB[2 * dc + 1][:, 0:512], w,
                                        hh[:, c2 * GTOK + 512:(c2 + 1) * GTOK],
                                        start=(c2 == 0), stop=(c2 == NC2 - 1))
                    return last
                pe(fdown, reads=[f"wdw:{wpar}", f"hh:{c2}"],
                   writes=[BK[i] for i in range(8)], tag="down")

            for c2 in range(NC2 + AH):
                if c2 < NC2:
                    _wd_dma(c2)
                if c2 >= AH:
                    _wd_mm(c2 - AH)
            for dc in range(4):
                ec = p * 4 + dc
                pt_reads = [f"pt:{ec}:{tt}" for tt in range(GTILES)]
                for qr in range(4):
                    sl_o = out_sb[:, (qr % 2) * 256:(qr % 2 + 1) * 256]
                    dve(lambda v, dc=dc, ec=ec, qr=qr, sl_o=sl_o:
                        v.scalar_tensor_tensor(
                            sl_o,
                            PB[2 * dc + qr // 2][:, (qr % 2) * 256:
                                                 (qr % 2 + 1) * 256],
                            1.0 / 127.0,
                            partial[:, ec * GTOK + qr * 256:
                                    ec * GTOK + (qr + 1) * 256],
                            ALU.mult, ALU.add),
                        reads=[BK[2 * dc + qr // 2]] + pt_reads,
                        writes=[f"out_sb:{qr % 2}"], tag="outevac")

                    def fst(sy, ec=ec, seq=seq, goff=goff, qr=qr, sl_o=sl_o):
                        dst = bass.AP(dr["out_fm"].tensor,
                                      (seq * D + ec * 128) * l_seq + goff +
                                      qr * 256,
                                      [[l_seq, 128], [1, 256]])
                        return sy.dma_start(dst, sl_o)
                    sdma(fst, reads=[f"out_sb:{qr % 2}"],
                         tag=f"store:{gi}:{ec}:{qr}")

    loaded = [False] * ntile

    def ensure_load(t):
        if 0 <= t < ntile and not loaded[t]:
            loaded[t] = True
            x_load(t)

    ensure_load(0)
    ensure_load(1)
    ln_stage(0)
    for tile in range(ntile):
        nxt = tile + 1
        ensure_load(nxt)
        pass_a_qz(tile)
        evac_q(tile)
        gates_pre(tile)
        rope_q(tile)
        issue_tq(tile)
        pass_a_k(tile)
        evac_k(tile)
        gates_mms_a(tile)
        if nxt < ntile and nxt % GTILES != 0:
            ln_square(nxt)
        rope_k_core(tile)
        issue_tk(tile)
        pass_a_v(tile)
        gates_mms_b(tile)
        kw_fold(tile)
        evac_v(tile)
        thunks = ln_thunks(nxt) if (nxt < ntile and nxt % GTILES != 0) else []
        nslots = NH + 5
        npc = (len(thunks) + nslots - 1) // nslots if thunks else 0
        idx = 0
        for s in range(nslots):
            if s < NH:
                scan_mm(tile, s)
            if 0 <= s - 1 < NH:
                scan_mneg(tile, s - 1)
            if 0 <= s - 2 < NH:
                scan_exp(tile, s - 2)
            if 0 <= s - 3 < NH:
                scan_shat(tile, s - 3)
            if 0 <= s - 4 < NH:
                scan_y(tile, s - 4)
            if 0 <= s - 5 < NH:
                scan_kv(tile, s - 5)
            for _ in range(npc):
                if idx < len(thunks):
                    thunks[idx]()
                    idx += 1
            if s < NH:
                scan_s(tile, s)
        while idx < len(thunks):
            thunks[idx]()
            idx += 1
        outproj(tile)
        while idx < len(thunks):
            thunks[idx]()
            idx += 1
        if nxt % GTILES == 0:
            ffn_group(tile // GTILES)
            if nxt < ntile:
                ln_stage(nxt)

    with nc.Block() as block:
        def emit(eng_obj, lst):
            for (fn, waits, sem, amt, tag) in lst:
                for (s, v) in waits:
                    eng_obj.wait_ge(s, v)
                ins = fn(eng_obj)
                ins.then_inc(sem, amt)

        @block.tensor
        def _(t):
            emit(t, P.q["pe"])

        @block.vector
        def _(v):
            emit(v, P.q["dve"])

        @block.scalar
        def _(s):
            emit(s, P.q["act"])

        @block.gpsimd
        def _(g):
            emit(g, P.q["gps"])

        @block.sync
        def _(sy):
            emit(sy, P.q["sp"])
            for (s, v) in P.final_waits("sp"):
                sy.wait_ge(s, v)

    return nc


def _layout_inputs(inputs, n_seq, l_seq, n_cores):
    f = np.ascontiguousarray
    x = np.asarray(inputs["L0_attn_x"], dtype=np.float32)
    base = {
        "wq_t": f(np.asarray(inputs["L0_attn_Wq"], np.float32).T),
        "wk_t": f(np.asarray(inputs["L0_attn_Wk"], np.float32).T),
        "wv_t": f(np.asarray(inputs["L0_attn_Wv"], np.float32).T),
        "wout_t": f(np.asarray(inputs["L0_attn_Wout"], np.float32).T),
        "wg_all": f(np.concatenate(
            [np.asarray(inputs["L0_attn_Wbeta"], np.float32).T,
             np.asarray(inputs["L0_attn_Wig"], np.float32).T,
             np.asarray(inputs["L0_attn_Wog"], np.float32).T], axis=1)),
        "b24": f(np.concatenate(
            [np.asarray(inputs["L0_attn_bbeta"], np.float32),
             np.asarray(inputs["L0_attn_big"], np.float32),
             np.asarray(inputs["L0_attn_bog"], np.float32)])[None, :]),
        "bout": f(np.asarray(inputs["L0_attn_bout"], np.float32)[None, :]),
        "nw": f(np.asarray(inputs["L0_attn_norm_w"], np.float32)[None, :]),
        "wgate_t": f(np.asarray(inputs["L0_ffn_w_gate"], np.int32).T),
        "wup_t": f(np.asarray(inputs["L0_ffn_w_up"], np.int32).T),
        "wdown_t": f(np.asarray(inputs["L0_ffn_w_down"], np.int32).T),
    }
    in_maps = []
    for c in range(n_cores):
        xs = x[c * n_seq:(c + 1) * n_seq, :l_seq, :]
        m = dict(base)
        m["x_fm"] = f(xs.transpose(0, 2, 1)).reshape(n_seq * D, l_seq)
        in_maps.append(m)
    return in_maps


def kernel(**inputs):
    n_seq, l_seq, n_cores = 2, 4096, 8
    in_maps = _layout_inputs(inputs, n_seq, l_seq, n_cores)
    nc = build_program(n_seq=n_seq, l_seq=l_seq)
    res = run_bass_kernel_spmd(nc, in_maps, core_ids=list(range(n_cores)))
    outs = []
    for c in range(n_cores):
        o = np.asarray(res.results[c]["out_fm"], dtype=np.float32)
        o = o.reshape(n_seq, D, l_seq).transpose(0, 2, 1)
        outs.append(o)
    return np.ascontiguousarray(np.concatenate(outs, axis=0))


# version 27
# speedup vs baseline: 1.0492x; 1.0492x over previous
import math
import numpy as np
import concourse.bass as bass
import concourse.mybir as mybir
from concourse.bass_utils import run_bass_kernel_spmd

F32 = mybir.dt.float32
BF16 = mybir.dt.bfloat16
I32 = mybir.dt.int32
AF = mybir.ActivationFunctionType
ALU = mybir.AluOpType

D = 1024
NH = 8
DH = 128
HALF = 64
H2 = 3072
NC2 = H2 // 128
NK = D // 128
TILE = 128
GTILES = 8
GTOK = TILE * GTILES
SCALE = 1.0 / math.sqrt(DH)
LNSCALE = math.log(SCALE)
THETA = 10000.0
TWO_PI = 2.0 * math.pi
_c1 = np.float32(6.28125)
_c2r = np.float64(TWO_PI) - np.float64(_c1)
_c2 = np.frombuffer((np.frombuffer(np.float32(_c2r).tobytes(),
                                   dtype=np.uint32) &
                     np.uint32(0xFFFFF000)).tobytes(), dtype=np.float32)[0]
_c3 = np.float32(np.float64(TWO_PI) - np.float64(_c1) - np.float64(_c2))
C1, C2, C3 = float(_c1), float(_c2), float(_c3)


class _Prog:

    ENGINES = ("pe", "dve", "act", "gps", "sp")

    def __init__(self, nc):
        self.nc = nc
        self.q = {e: [] for e in self.ENGINES}
        self.sem = {}
        for name in ("s_pe", "s_dve", "s_act", "s_gpc", "s_sp"):
            self.sem[name] = [nc.alloc_semaphore(name), 0]
        self.esem = {"pe": "s_pe", "dve": "s_dve", "act": "s_act",
                     "gps": "s_gpc", "sp": "s_sp"}
        self.dring = {}
        for qn, n in (("gps", 10), ("sp", 8), ("act", 6)):
            for j in range(n):
                nm = f"s_d{qn}{j}"
                self.sem[nm] = [nc.alloc_semaphore(nm), 0]
            self.dring[qn] = (n, [0])
        self.buf = {}
        self.clock = {e: {} for e in self.ENGINES}
        self.barrier_tok = None
        self.barrier_pending = set()

    def op(self, eng, fn, reads=(), writes=(), dma=False, tag=""):
        waits = {}
        if dma:
            nring, cnt = self.dring[eng if eng != "actq" else "act"]
            j = cnt[0] % nring
            cnt[0] += 1
            sname = f"s_d{eng if eng != 'actq' else 'act'}{j}"
            amt = 16
            if self.sem[sname][1] > 0:
                waits[sname] = self.sem[sname][1]
        else:
            sname = self.esem[eng]
            amt = 1
        if eng == "actq":
            eng = "act"
        if eng in self.barrier_pending:
            self.barrier_pending.discard(eng)
            for s, v in self.barrier_tok.items():
                waits[s] = max(waits.get(s, 0), v)
        for b in reads:
            e = self.buf.get(b)
            if e and e["w"] is not None:
                s, v = e["w"]
                waits[s] = max(waits.get(s, 0), v)
        for b in writes:
            e = self.buf.get(b)
            if e:
                if e["w"] is not None:
                    s, v = e["w"]
                    waits[s] = max(waits.get(s, 0), v)
                for (s, v) in e["r"]:
                    waits[s] = max(waits.get(s, 0), v)
        ck = self.clock[eng]
        wl = [(s, v) for (s, v) in waits.items() if ck.get(s, -1) < v]
        for (s, v) in wl:
            ck[s] = v
        self.sem[sname][1] += amt
        tok = (sname, self.sem[sname][1])
        self.q[eng].append((fn, [(self.sem[s][0], v) for (s, v) in wl],
                            self.sem[sname][0], amt, tag))
        for b in reads:
            self.buf.setdefault(b, {"w": None, "r": []})["r"].append(tok)
        for b in writes:
            self.buf[b] = {"w": tok, "r": []}
        return tok

    def barrier(self):
        tot = {}
        for e in self.buf.values():
            for tok in ([e["w"]] if e["w"] else []) + e["r"]:
                s, v = tok
                tot[s] = max(tot.get(s, 0), v)
        self.barrier_tok = tot
        self.barrier_pending = set(self.ENGINES)

    def final_waits(self, eng):
        out = {}
        for e in self.buf.values():
            for tok in ([e["w"]] if e["w"] else []) + e["r"]:
                s, v = tok
                out[s] = max(out.get(s, 0), v)
        ck = self.clock[eng]
        return [(self.sem[s][0], v) for s, v in out.items()
                if ck.get(s, -1) < v]


def build_program(n_seq=2, l_seq=4096):
    ntile = n_seq * l_seq // TILE
    ngroup = ntile // GTILES
    tps = l_seq // TILE
    nct = tps
    ntab = nct * HALF
    nc = bass.Bass("TRN2", target_bir_lowering=False, debug=False)

    dr = {}
    dr["x_fm"] = nc.dram_tensor("x_fm", [n_seq * D, l_seq], F32,
                                kind="ExternalInput").ap()
    for nm in ("wq_t", "wk_t", "wv_t", "wout_t"):
        dr[nm] = nc.dram_tensor(nm, [D, D], F32, kind="ExternalInput").ap()
    dr["wg_all"] = nc.dram_tensor("wg_all", [D, 24], F32,
                                  kind="ExternalInput").ap()
    dr["b24"] = nc.dram_tensor("b24", [1, 24], F32, kind="ExternalInput").ap()
    dr["bout"] = nc.dram_tensor("bout", [1, D], F32, kind="ExternalInput").ap()
    dr["nw"] = nc.dram_tensor("nw", [1, D], F32, kind="ExternalInput").ap()
    dr["wgate_t"] = nc.dram_tensor("wgate_t", [D, H2], I32,
                                   kind="ExternalInput").ap()
    dr["wup_t"] = nc.dram_tensor("wup_t", [D, H2], I32,
                                 kind="ExternalInput").ap()
    dr["wdown_t"] = nc.dram_tensor("wdown_t", [H2, D], I32,
                                   kind="ExternalInput").ap()
    dr["out_fm"] = nc.dram_tensor("out_fm", [n_seq * D, l_seq], F32,
                                  kind="ExternalOutput").ap()
    dr["wg_c"] = nc.dram_tensor("wg_c", [D, H2], BF16, kind="Internal").ap()
    dr["wu_c"] = nc.dram_tensor("wu_c", [D, H2], BF16, kind="Internal").ap()
    dr["wd_c"] = nc.dram_tensor("wd_c", [H2, D], BF16, kind="Internal").ap()

    PB = [nc.alloc_psum_tensor(f"PB{i}", [128, 512], F32) for i in range(8)]
    BK = [f"@B{i}" for i in range(8)]

    P = _Prog(nc)
    gop = lambda fn, reads=(), writes=(), tag="": P.op(
        "gps", fn, reads, writes, tag=tag)
    gdma = lambda fn, reads=(), writes=(), tag="": P.op(
        "gps", fn, reads, writes, dma=True, tag=tag)
    sdma = lambda fn, reads=(), writes=(), tag="": P.op(
        "sp", fn, reads, writes, dma=True, tag=tag)
    adma = lambda fn, reads=(), writes=(), tag="": P.op(
        "actq", fn, reads, writes, dma=True, tag=tag)
    pe = lambda fn, reads=(), writes=(), tag="": P.op(
        "pe", fn, reads, writes, tag=tag)
    dve = lambda fn, reads=(), writes=(), tag="": P.op(
        "dve", fn, reads, writes, tag=tag)
    act = lambda fn, reads=(), writes=(), tag="": P.op(
        "act", fn, reads, writes, tag=tag)

    sbal = nc.alloc_sbuf_tensor

    cos_bf = sbal("cos_bf", [128, ntab], BF16)
    sin_bf = sbal("sin_bf", [128, ntab], BF16)
    tri_bf = sbal("tri_bf", [128, 128], BF16)
    mneg = sbal("mneg", [128, 128], BF16)
    ident = sbal("ident", [128, 128], BF16)
    ident_f = sbal("ident_f", [1, 1], F32)
    negsel = sbal("negsel", [8, NH * 128], BF16)
    ones_c = sbal("ones_c", [128, 1], BF16)
    ones_r = sbal("ones_r", [1, 128], BF16)
    nones_r = sbal("nones_r", [1, 128], BF16)
    cp1 = sbal("cp1", [128, 1], F32)
    cpeps = sbal("cpeps", [1, 2], F32)
    nw_col = sbal("nw_col", [128, NK], F32)
    bout_c = sbal("bout_c", [128, NK], F32)
    b24_t = sbal("b24_t", [128, 24], F32)
    b24_bf = sbal("b24_bf", [1, 24], BF16)

    gop(lambda g: g.memset(tri_bf[:], 1.0), writes=["tri*"])
    gop(lambda g: g.affine_select(tri_bf[:], tri_bf[:], [[1, 128]],
                                  ALU.is_ge, 0.0, base=0,
                                  channel_multiplier=-1),
        reads=["tri*"], writes=["tri"])
    gop(lambda g: g.memset(mneg[:], 0.0), writes=["mneg*"])
    gop(lambda g: g.affine_select(mneg[:], mneg[:], [[1, 128]],
                                  ALU.is_ge, -1.0e30, base=0,
                                  channel_multiplier=-1),
        reads=["mneg*"], writes=["mneg"])
    gop(lambda g: g.memset(ident[:], 1.0), writes=["ident*"])
    gop(lambda g: g.affine_select(ident[:], ident[:], [[1, 128]],
                                  ALU.is_equal, 0.0, base=0,
                                  channel_multiplier=-1),
        reads=["ident*"], writes=["ident"])
    gop(lambda g: g.memset(ident_f[:], 1.0), writes=["identf"])
    gop(lambda g: g.memset(negsel[:], -1.0), writes=["negsel*"])
    gop(lambda g: g.affine_select(negsel[:], negsel[:], [[1, NH], [0, 128]],
                                  ALU.is_equal, 0.0, base=0,
                                  channel_multiplier=-1),
        reads=["negsel*"], writes=["negsel"])
    gop(lambda g: g.memset(ones_c[:], 1.0), writes=["ones_c"])
    gop(lambda g: g.memset(ones_r[:], 1.0), writes=["ones_r"])
    gop(lambda g: g.memset(nones_r[:], -1.0), writes=["nones_r"])
    gop(lambda g: g.memset(cp1[:], 1.0), writes=["cp1"])
    gop(lambda g: g.memset(cphp[:], math.pi / 2.0), writes=["cphp"])
    gop(lambda g: g.memset(cpeps[:, 0:1], 1.0e-5), writes=["cpeps"])

    gdma(lambda g: g.dma_start(
        nw_col[:], bass.AP(dr["nw"].tensor, 0, [[1, 128], [128, NK]]),
        allow_slow_non_contiguous=True),
        writes=["nw_col"])
    gdma(lambda g: g.dma_start(
        bout_c[:], bass.AP(dr["bout"].tensor, 0, [[1, 128], [128, NK]]),
        allow_slow_non_contiguous=True),
        writes=["bout_c"])
    gdma(lambda g: g.dma_start(b24_bf[:], dr["b24"]), writes=["b24bf"])
    pe(lambda t: t.matmul(PB[6][:, 0:24], ones_r[:], b24_bf[:],
                          start=True, stop=True),
       reads=["ones_r", "b24bf"], writes=[BK[6]], tag="b24bc")
    dve(lambda v: v.tensor_copy(b24_t[:], PB[6][:, 0:24]),
        reads=[BK[6]], writes=["b24"])

    wq = sbal("wq", [128, NK * D], BF16)
    wk = sbal("wk", [128, NK * D], BF16)
    wv = sbal("wv", [128, NK * D], BF16)
    wgs = sbal("wgs", [128, NK * 24], BF16)
    wo = sbal("wo", [128, NK * D], BF16)

    def wload(dst, src_name, width, bufname):
        def fn(g, dst=dst, src_name=src_name, width=width):
            src = bass.AP(dr[src_name].tensor, 0,
                          [[width, 128], [width * 128, NK], [1, width]])
            return g.dma_start(
                dst[:].rearrange("p (k w) -> p k w", k=NK), src)
        gdma(fn, writes=[bufname], tag=f"load:{bufname}")

    wload(wq, "wq_t", D, "wq")
    wload(wk, "wk_t", D, "wk")
    wload(wv, "wv_t", D, "wv")
    wload(wgs, "wg_all", 24, "wgs")
    wload(wo, "wout_t", D, "wo")
    for (cn, sn, rows) in (("wg_c", "wgate_t", D), ("wu_c", "wup_t", D),
                           ("wd_c", "wdown_t", H2)):
        for hhf in range(2):
            gdma(lambda g, cn=cn, sn=sn, rows=rows, hhf=hhf: g.dma_start(
                dr[cn].rearrange("a b -> a b")[hhf * (rows // 2):
                                               (hhf + 1) * (rows // 2), :],
                dr[sn].rearrange("a b -> a b")[hhf * (rows // 2):
                                               (hhf + 1) * (rows // 2), :]),
                writes=[f"{cn}:{hhf}"], tag=f"cache:{cn}:{hhf}")


    from contextlib import ExitStack
    _scratch = ExitStack()

    def scr(name, shape, dtype):
        return _scratch.enter_context(nc.sbuf_tensor(name, shape, dtype))

    tb_f = scr("tb_f", [128, ntab], F32)
    tb_g = scr("tb_g", [128, ntab], F32)
    tb_h = scr("tb_h", [128, ntab], F32)
    tb_i = scr("tb_i", [128, ntab], I32)
    r0f = scr("r0f", [1, ntab], F32)
    r1f = scr("r1f", [1, ntab], F32)
    r0h = scr("r0h", [1, ntab], BF16)
    r0l = scr("r0l", [1, ntab], BF16)
    r1h = scr("r1h", [1, ntab], BF16)
    r1l = scr("r1l", [1, ntab], BF16)
    tvals = scr("tvals", [1, 128], BF16)
    invf = scr("invf", [1, HALF], F32)
    cphp = scr("cphp", [128, 1], F32)
    t_iot = scr("t_iot", [1, 128], F32)

    gop(lambda g: g.iota(t_iot[:], [[1, 128]], base=0, channel_multiplier=0,
                         allow_small_or_imprecise_dtypes=True),
        writes=["iot"])
    act(lambda s: s.activation(invf[:], t_iot[0:1, 0:HALF], AF.Exp,
                               scale=-math.log(THETA) / HALF),
        reads=["iot"], writes=["invf"])
    dve(lambda v: v.tensor_copy(tvals[:], t_iot[:]),
        reads=["iot"], writes=["tvals"])
    gop(lambda g: g.iota(r0f[:], [[1, nct], [0, HALF]], base=0,
                         channel_multiplier=0,
                         allow_small_or_imprecise_dtypes=True),
        writes=["r0f"])

    def _rhs0(v):
        invb = bass.AP(invf, 0, [[HALF, 1], [0, nct], [1, HALF]])
        return v.tensor_tensor(
            r0f[:].rearrange("o (c i) -> o c i", i=HALF),
            r0f[:].rearrange("o (c i) -> o c i", i=HALF),
            invb, ALU.mult)
    dve(_rhs0, reads=["r0f", "invf"], writes=["r0f"])
    dve(lambda v: v.tensor_scalar_mul(r0f[:], r0f[:], 128.0),
        reads=["r0f"], writes=["r0f"])

    def _rhs1(v):
        invb = bass.AP(invf, 0, [[HALF, 1], [0, nct], [1, HALF]])
        return v.tensor_copy(r1f[:].rearrange("o (c i) -> o c i", i=HALF),
                             invb)
    dve(_rhs1, reads=["invf"], writes=["r1f"])
    dve(lambda v: v.tensor_copy(r0h[:], r0f[:]), reads=["r0f"],
        writes=["r0h"])
    dve(lambda v: v.tensor_tensor(r0l[:], r0f[:], r0h[:], ALU.subtract),
        reads=["r0f", "r0h"], writes=["r0l"])
    dve(lambda v: v.tensor_copy(r1h[:], r1f[:]), reads=["r1f"],
        writes=["r1h"])
    dve(lambda v: v.tensor_tensor(r1l[:], r1f[:], r1h[:], ALU.subtract),
        reads=["r1f", "r1h"], writes=["r1l"])
    nq = min(ntab, 512)
    nmm = (ntab + nq - 1) // nq
    for i in range(nmm):
        bk = i % 4
        sl_ = slice(i * nq, (i + 1) * nq)

        def fmm(t, sl_=sl_, bk=bk):
            t.matmul(PB[bk][:, 0:nq], ones_r[:], r0h[0:1, sl_],
                     start=True, stop=False)
            t.matmul(PB[bk][:, 0:nq], ones_r[:], r0l[0:1, sl_],
                     start=False, stop=False)
            t.matmul(PB[bk][:, 0:nq], tvals[:], r1h[0:1, sl_],
                     start=False, stop=False)
            return t.matmul(PB[bk][:, 0:nq], tvals[:], r1l[0:1, sl_],
                            start=False, stop=True)
        pe(fmm, reads=["ones_r", "tvals", "r0h", "r0l", "r1h", "r1l"],
           writes=[BK[bk]], tag="freqmm")
        dve(lambda v, sl_=sl_, bk=bk: v.tensor_copy(tb_f[:, sl_],
                                                    PB[bk][:, 0:nq]),
            reads=[BK[bk]], writes=["tb_f"])
    dve(lambda v: v.tensor_scalar(tb_g[:], tb_f[:], 1.0 / TWO_PI, 0.5,
                                  ALU.mult, ALU.add),
        reads=["tb_f"], writes=["tb_g"])
    dve(lambda v: v.tensor_copy(tb_i[:], tb_g[:]),
        reads=["tb_g"], writes=["tb_i"])
    dve(lambda v: v.tensor_copy(tb_g[:], tb_i[:]),
        reads=["tb_i"], writes=["tb_g"])
    for c in (C1, C2, C3):
        dve(lambda v, c=c: v.scalar_tensor_tensor(
            tb_f[:], tb_g[:], -c, tb_f[:], ALU.mult, ALU.add),
            reads=["tb_f", "tb_g"], writes=["tb_f"])
    dve(lambda v: v.tensor_scalar_min(tb_f[:], tb_f[:], math.pi),
        reads=["tb_f"], writes=["tb_f"])
    dve(lambda v: v.tensor_scalar_max(tb_f[:], tb_f[:], -math.pi),
        reads=["tb_f"], writes=["tb_f"])
    act(lambda s: s.activation(tb_h[:], tb_f[:], AF.Sin),
        reads=["tb_f"], writes=["tb_h"])
    dve(lambda v: v.tensor_copy(sin_bf[:], tb_h[:]),
        reads=["tb_h"], writes=["sin"])
    act(lambda s: s.activation(tb_g[:], tb_f[:], AF.Abs),
        reads=["tb_f"], writes=["tb_g"])
    act(lambda s: s.activation(tb_h[:], tb_g[:], AF.Sin, scale=-1.0,
                               bias=cphp[:, 0:1]),
        reads=["tb_g", "cphp"], writes=["tb_h"])
    dve(lambda v: v.tensor_copy(cos_bf[:], tb_h[:]),
        reads=["tb_h"], writes=["cos"])

    _scratch.close()

    wg_win = sbal("wg_win", [128, 2 * D], BF16)
    wu_win = sbal("wu_win", [128, 2 * D], BF16)
    wd_win = sbal("wd_win", [128, 4 * 512], BF16)
    xn_fm = sbal("xn_fm", [128, NK * GTOK], BF16)
    partial = sbal("partial", [128, NK * GTOK], BF16)
    hh = sbal("hh", [128, NC2 * GTOK], BF16)
    xf_st = sbal("xf_st", [128, 2 * D], BF16)
    out_sb = sbal("out_sb", [128, 2 * 256], F32)
    q_tm = sbal("q_tm", [128, D], BF16)
    k_tm = sbal("k_tm", [128, D], BF16)
    v_tm = sbal("v_tm", [128, D], BF16)
    kw_tm = sbal("kw_tm", [128, D], BF16)
    xnw_st = sbal("xnw_st", [128, NK * 128], BF16)
    kfm = sbal("kfm", [128, 8 * 128], BF16)
    qofm = sbal("qofm", [128, 8 * 128], BF16)
    qhfm = sbal("qhfm", [128, 4 * 128], BF16)
    shat = sbal("shat", [128, 4 * 128], BF16)
    atile = sbal("atile", [128, 4 * 128], BF16)
    dtile = sbal("dtile", [128, 4 * 128], BF16)
    mbf = sbal("mbf", [128, 4 * 128], BF16)
    yfm = sbal("yfm", [128, NH * 128], BF16)
    msb = sbal("msb", [128, n_seq * NH * 128], F32)
    zrow = sbal("zrow", [128, 24], F32)
    gsc = sbal("gsc", [128, 24], F32)
    spb = sbal("spb", [128, 16], F32)
    sp_hi = sbal("sp_hi", [128, 16], BF16)
    sp_lo = sbal("sp_lo", [128, 16], BF16)
    p_tm = sbal("p_tm", [128, 8], F32)
    pg_hi = sbal("pg_hi", [8, 128], BF16)
    pg_lo = sbal("pg_lo", [8, 128], BF16)
    pc_hi = sbal("pc_hi", [1, 8], BF16)
    pc_lo = sbal("pc_lo", [1, 8], BF16)
    b2p = sbal("b2p", [128, 8], F32)
    w_tm = sbal("w_tm", [128, 8], BF16)
    dc_t = sbal("dc_t", [128, 8], F32)
    wargt = sbal("wargt", [128, 8], F32)
    srow = sbal("srow", [1, 4 * 128], F32)
    srow_bf = sbal("srow_bf", [1, 2 * 128], BF16)
    acol = sbal("acol", [128, 1], F32)
    s_sb = sbal("s_sb", [128, 128], BF16)

    gop(lambda g: g.memset(msb[:], 0.0),
        writes=[f"M:{i}" for i in range(n_seq * NH)])

    P.barrier()

    def x_load(tile):
        par = tile % 2
        seq = tile // tps
        c = tile % tps

        def fn(g, par=par, seq=seq, c=c):
            src = bass.AP(dr["x_fm"].tensor, seq * D * l_seq + c * TILE,
                          [[l_seq, 128], [l_seq * 128, NK], [1, TILE]])
            return g.dma_start(
                xf_st[:, par * D:(par + 1) * D].rearrange(
                    "p (k t) -> p k t", k=NK), src)
        gdma(fn, writes=[f"xf:{par}"], tag=f"xload:{tile}")

    def R(i):
        return srow[0:1, i * 128:(i + 1) * 128]

    def ln_thunks(tile):
        TH = []

        def dpe(*a, **kw):
            TH.append(lambda a=a, kw=kw: pe(*a, **kw))

        def ddve(*a, **kw):
            TH.append(lambda a=a, kw=kw: dve(*a, **kw))

        def dact(*a, **kw):
            TH.append(lambda a=a, kw=kw: act(*a, **kw))
        _ln_body(tile, dpe, ddve, dact, square=False)
        return TH

    def ln_square(tile):
        par = tile % 2
        ti = tile % GTILES
        xf = xf_st[:, par * D:(par + 1) * D]
        toff0 = ti * TILE
        sq_names = [f"pt:{k}:{ti}" for k in range(NK)]
        sqfull = bass.AP(partial.tensor if hasattr(partial, "tensor")
                         else partial, toff0,
                         [[NK * GTOK, 128], [GTOK, NK], [1, TILE]])
        dve(lambda v: v.tensor_tensor(sqfull, xf[:].rearrange(
            "p (k t) -> p k t", k=NK), xf[:].rearrange(
            "p (k t) -> p k t", k=NK), ALU.mult),
            reads=[f"xf:{par}"], writes=sq_names)

    def ln_stage(tile):
        ln_square(tile)
        for th in ln_thunks(tile):
            th()

    def _ln_body(tile, pe, dve, act, square=True):
        par = tile % 2
        ti = tile % GTILES
        xf = xf_st[:, par * D:(par + 1) * D]

        def sx(t):
            last = None
            for k in range(NK):
                last = t.matmul(PB[0][0:1, 0:TILE], ones_c[:],
                                xf[:, k * TILE:(k + 1) * TILE],
                                start=(k == 0), stop=(k == NK - 1))
            return last
        pe(sx, reads=[f"xf:{par}", "ones_c"], writes=[BK[0]], tag="sxmm")
        toff0 = ti * TILE
        sq_names = [f"pt:{k}:{ti}" for k in range(NK)]
        if square:
            sqfull = bass.AP(partial.tensor if hasattr(partial, "tensor")
                             else partial, toff0,
                             [[NK * GTOK, 128], [GTOK, NK], [1, TILE]])
            dve(lambda v: v.tensor_tensor(sqfull, xf[:].rearrange(
                "p (k t) -> p k t", k=NK), xf[:].rearrange(
                "p (k t) -> p k t", k=NK), ALU.mult),
                reads=[f"xf:{par}"], writes=sq_names)
        for k in range(NK):
            sqap = partial[:, k * GTOK + toff0:k * GTOK + toff0 + TILE]
            pe(lambda t, k=k, sqap=sqap: t.matmul(
                PB[1][0:1, 0:TILE], ones_c[:], sqap,
                start=(k == 0), stop=(k == NK - 1)),
                reads=[f"pt:{k}:{ti}", "ones_c"], writes=[BK[1]], tag="sxxmm")
        dve(lambda v: v.tensor_scalar_mul(R(0), PB[0][0:1, 0:TILE], 1.0 / D),
            reads=[BK[0]], writes=["srow"])
        dve(lambda v: v.tensor_copy(srow_bf[0:1, 0:128], R(0)),
            reads=["srow"], writes=["srbf"])
        dve(lambda v: v.tensor_tensor(R(1), R(0), R(0), ALU.mult),
            reads=["srow"], writes=["srow"])
        dve(lambda v: v.scalar_tensor_tensor(R(1), PB[1][0:1, 0:TILE],
                                             1.0 / D, R(1),
                                             ALU.mult, ALU.subtract),
            reads=[BK[1], "srow"], writes=["srow"])
        act(lambda s: s.activation(R(2), R(1), AF.Ln, bias=cpeps[0:1, 0:1]),
            reads=["srow", "cpeps"], writes=["srow"])
        act(lambda s: s.activation(R(3), R(2), AF.Exp, scale=-0.5),
            reads=["srow"], writes=["srow"])
        dve(lambda v: v.tensor_copy(srow_bf[0:1, 128:256], R(3)),
            reads=["srow"], writes=["srbf"])
        act(lambda s: s.activation(R(0), R(2), AF.Exp, scale=-1.0),
            reads=["srow"], writes=["srow"])
        dve(lambda v: v.tensor_tensor(R(3), R(1), R(0), ALU.mult),
            reads=["srow"], writes=["srow"])
        dve(lambda v: v.tensor_scalar_add(R(3), R(3), 1.0e-6),
            reads=["srow"], writes=["srow"])
        act(lambda s: s.activation(R(2), R(3), AF.Ln),
            reads=["srow"], writes=["srow"])
        act(lambda s: s.activation(R(0), R(2), AF.Exp, scale=-0.5),
            reads=["srow"], writes=["srow"])
        pe(lambda t: t.transpose(PB[0][:, 128:129],
                                 R(0), ident_f[0:1, 0:1]),
           reads=["srow", "identf"], writes=[BK[0]], tag="alphaT")
        dve(lambda v: v.tensor_copy(acol[:], PB[0][:, 128:129]),
            reads=[BK[0]], writes=["acol"])
        pe(lambda t: t.matmul(PB[0][:, 256:384], ones_r[:],
                              srow_bf[0:1, 128:256], start=True, stop=True),
           reads=["ones_r", "srbf"], writes=[BK[0]], tag="sbc")
        act(lambda s: s.activation(s_sb[:], PB[0][:, 256:384], AF.Copy),
            reads=[BK[0]], writes=["s_sb"])
        toff = ti * TILE
        for k in range(NK):
            bk = k % 2
            pe(lambda t, k=k, bk=bk: (
                t.matmul(PB[bk][:, 384:512], ident[:],
                         xf[:, k * TILE:(k + 1) * TILE],
                         start=True, stop=False),
                t.matmul(PB[bk][:, 384:512], nones_r[:],
                         srow_bf[0:1, 0:128], start=False, stop=True))[-1],
                reads=["ident", f"xf:{par}", "nones_r", "srbf"],
                writes=[BK[bk]], tag="psumx")
            dve(lambda v, k=k, bk=bk, toff=toff: v.tensor_tensor(
                xn_fm[:, k * GTOK + toff:k * GTOK + toff + TILE],
                PB[bk][:, 384:512], s_sb[:], ALU.mult),
                reads=[BK[bk], "s_sb"], writes=[f"xn:{k}:{ti}"], tag="xn")
            dve(lambda v, k=k, toff=toff: v.tensor_scalar(
                xnw_st[:, k * 128:(k + 1) * 128],
                xn_fm[:, k * GTOK + toff:k * GTOK + toff + TILE],
                nw_col[:, k:k + 1], None, ALU.mult),
                reads=[f"xn:{k}:{ti}", "nw_col"], writes=[f"xnw:{k}"],
                tag="xnw")

    def pass_a_qz(tile):
        for k in range(NK):
            xk = xnw_st[:, k * 128:(k + 1) * 128]
            rd = [f"xnw:{k}", "wq", "wgs"]

            def fq(t, k=k, xk=xk):
                t.matmul(PB[0][:, 0:512], xk, wq[:, k * D:k * D + 512],
                         start=(k == 0), stop=(k == NK - 1))
                return t.matmul(PB[1][:, 0:512], xk,
                                wq[:, k * D + 512:(k + 1) * D],
                                start=(k == 0), stop=(k == NK - 1))
            pe(fq, reads=rd, writes=[BK[0], BK[1]], tag="mmQ")
            pe(lambda t, k=k, xk=xk: t.matmul(
                PB[6][:, 0:24], xk, wgs[:, k * 24:(k + 1) * 24],
                start=(k == 0), stop=(k == NK - 1)),
                reads=rd, writes=[BK[6]], tag="mmZ")

    def pass_a_k(tile):
        for k in range(NK):
            xk = xnw_st[:, k * 128:(k + 1) * 128]
            rd = [f"xnw:{k}", "wk"]

            def fk(t, k=k, xk=xk):
                t.matmul(PB[2][:, 0:512], xk, wk[:, k * D:k * D + 512],
                         start=(k == 0), stop=(k == NK - 1))
                return t.matmul(PB[3][:, 0:512], xk,
                                wk[:, k * D + 512:(k + 1) * D],
                                start=(k == 0), stop=(k == NK - 1))
            pe(fk, reads=rd, writes=[BK[2], BK[3]], tag="mmK")

    def pass_a_v(tile):
        for k in range(NK):
            xk = xnw_st[:, k * 128:(k + 1) * 128]
            rd = [f"xnw:{k}", "wv"]

            def fv(t, k=k, xk=xk):
                t.matmul(PB[4][:, 0:512], xk, wv[:, k * D:k * D + 512],
                         start=(k == 0), stop=(k == NK - 1))
                return t.matmul(PB[5][:, 0:512], xk,
                                wv[:, k * D + 512:(k + 1) * D],
                                start=(k == 0), stop=(k == NK - 1))
            pe(fv, reads=rd, writes=[BK[4], BK[5]], tag="mmV")

    def gates_pre(tile):
        dve(lambda v: v.tensor_scalar(zrow[:], PB[6][:, 0:24], acol[:, 0:1],
                                      None, ALU.mult),
            reads=[BK[6], "acol"], writes=["zrow"])
        dve(lambda v: v.tensor_tensor(zrow[:], zrow[:], b24_t[:], ALU.add),
            reads=["zrow", "b24"], writes=["zrow"])
        act(lambda s: s.activation(gsc[:], zrow[:], AF.Exp, scale=-1.0),
            reads=["zrow"], writes=["gscA"])
        act(lambda s: s.activation(gsc[:], gsc[:], AF.Ln, bias=cp1[:, 0:1]),
            reads=["gscA"], writes=["gscA"])
        dve(lambda v: v.tensor_scalar_min(spb[:, 0:8], gsc[:, 0:8], 80.0),
            reads=["gscA"], writes=["spb"])
        dve(lambda v: v.tensor_copy(spb[:, 8:16], gsc[:, 16:24]),
            reads=["gscA"], writes=["spb"])
        dve(lambda v: v.tensor_copy(sp_hi[:], spb[:]),
            reads=["spb"], writes=["sphi"])
        dve(lambda v: v.tensor_tensor(sp_lo[:], spb[:], sp_hi[:],
                                      ALU.subtract),
            reads=["spb", "sphi"], writes=["splo"])

    def gates_mms_a(tile):
        pe(lambda t: (
            t.matmul(PB[6][:, 32:40], tri_bf[:], sp_hi[:, 0:8],
                     start=True, stop=False),
            t.matmul(PB[6][:, 32:40], tri_bf[:], sp_lo[:, 0:8],
                     start=False, stop=True))[-1],
            reads=["tri", "sphi", "splo"], writes=[BK[6]], tag="Pmm")
        dve(lambda v: v.tensor_copy(p_tm[:], PB[6][:, 32:40]),
            reads=[BK[6]], writes=["p_tm"])
        pe(lambda t: (
            t.matmul(PB[7][0:8, 0:128], sp_hi[:, 0:8], tri_bf[:],
                     start=True, stop=False),
            t.matmul(PB[7][0:8, 0:128], sp_lo[:, 0:8], tri_bf[:],
                     start=False, stop=False),
            t.matmul(PB[7][0:8, 0:128], sp_hi[:, 8:16], ident[:],
                     start=False, stop=False),
            t.matmul(PB[7][0:8, 0:128], sp_lo[:, 8:16], ident[:],
                     start=False, stop=True))[-1],
            reads=["tri", "ident", "sphi", "splo"], writes=[BK[7]],
            tag="Pgmm")
        dve(lambda v: v.tensor_copy(pg_hi[:], PB[7][0:8, 0:128]),
            reads=[BK[7]], writes=["pghi"])
        dve(lambda v: v.tensor_tensor(pg_lo[:], PB[7][0:8, 0:128], pg_hi[:],
                                      ALU.subtract),
            reads=[BK[7], "pghi"], writes=["pglo"])
        pe(lambda t: (
            t.matmul(PB[6][0:1, 48:56], ones_c[:], sp_hi[:, 0:8],
                     start=True, stop=False),
            t.matmul(PB[6][0:1, 48:56], ones_c[:], sp_lo[:, 0:8],
                     start=False, stop=True))[-1],
            reads=["ones_c", "sphi", "splo"], writes=[BK[6]], tag="PCmm")
        dve(lambda v: v.tensor_copy(pc_hi[:], PB[6][0:1, 48:56]),
            reads=[BK[6]], writes=["pchi"])
        dve(lambda v: v.tensor_tensor(pc_lo[:], PB[6][0:1, 48:56], pc_hi[:],
                                      ALU.subtract),
            reads=[BK[6], "pchi"], writes=["pclo"])
        dve(lambda v: v.tensor_tensor(b2p[:], p_tm[:], gsc[:, 8:16],
                                      ALU.subtract),
            reads=["p_tm", "gscA"], writes=["b2p"])
        dve(lambda v: v.tensor_scalar_add(b2p[:], b2p[:], LNSCALE),
            reads=["b2p"], writes=["b2p"])

    def gates_mms_b(tile):
        pe(lambda t: (
            t.matmul(PB[6][:, 64:72], ones_r[:], pc_hi[:],
                     start=True, stop=False),
            t.matmul(PB[6][:, 64:72], ones_r[:], pc_lo[:],
                     start=False, stop=True))[-1],
            reads=["ones_r", "pchi", "pclo"], writes=[BK[6]], tag="PCbmm")
        act(lambda s: s.activation(dc_t[:], PB[6][:, 64:72], AF.Exp,
                                   scale=-1.0),
            reads=[BK[6]], writes=["dc"])
        dve(lambda v: v.tensor_tensor(wargt[:], b2p[:], PB[6][:, 64:72],
                                      ALU.subtract),
            reads=["b2p", BK[6]], writes=["wargt"])
        act(lambda s: s.activation(w_tm[:], wargt[:], AF.Exp),
            reads=["wargt"], writes=["w_tm"])

    def evac_q(tile):
        act(lambda s: s.activation(q_tm[:, 0:512], PB[0][:, 0:512], AF.Copy,
                                   scale=acol[:, 0:1]),
            reads=[BK[0], "acol"], writes=["q_tm"])
        act(lambda s: s.activation(q_tm[:, 512:1024], PB[1][:, 0:512],
                                   AF.Copy, scale=acol[:, 0:1]),
            reads=[BK[1], "acol"], writes=["q_tm"])

    def evac_k(tile):
        act(lambda s: s.activation(k_tm[:, 0:512], PB[2][:, 0:512], AF.Copy,
                                   scale=acol[:, 0:1]),
            reads=[BK[2], "acol"], writes=["k_tm"])
        act(lambda s: s.activation(k_tm[:, 512:1024], PB[3][:, 0:512],
                                   AF.Copy, scale=acol[:, 0:1]),
            reads=[BK[3], "acol"], writes=["k_tm"])

    def evac_v(tile):
        dve(lambda v: v.tensor_scalar(v_tm[:, 0:512],
                                      PB[4][:, 0:512], acol[:, 0:1], None,
                                      ALU.mult),
            reads=[BK[4], "acol"], writes=["v_tm"])
        dve(lambda v: v.tensor_scalar(v_tm[:, 512:D],
                                      PB[5][:, 0:512], acol[:, 0:1], None,
                                      ALU.mult),
            reads=[BK[5], "acol"], writes=["v_tm"])

    def _rope_one(tile, src_sb, srcb):
        c = tile % tps
        ti = tile % GTILES
        toff = ti * TILE
        cosap = bass.AP(cos_bf, c * HALF,
                        [[ntab, 128], [0, 4], [0, 2], [1, HALF]])
        sinap = bass.AP(sin_bf, c * HALF,
                        [[ntab, 128], [0, 4], [0, 2], [1, HALF]])
        lo = bass.AP(src_sb, 0, [[D, 128], [256, 4], [128, 2], [1, HALF]])
        hi = bass.AP(src_sb, HALF, [[D, 128], [256, 4], [128, 2], [1, HALF]])
        rsa = bass.AP(partial, toff, [[NK * GTOK, 128], [GTOK, 4],
                                      [HALF, 2], [1, HALF]])
        rsb = bass.AP(partial, 4 * GTOK + toff,
                      [[NK * GTOK, 128], [GTOK, 4], [HALF, 2], [1, HALF]])
        rsc = bass.AP(kw_tm, 0, [[D, 128], [128, 8], [1, HALF]])
        rsd = bass.AP(kw_tm, HALF, [[D, 128], [128, 8], [1, HALF]])
        lo3 = bass.AP(src_sb, 0, [[D, 128], [DH, NH], [1, HALF]])
        hi3 = bass.AP(src_sb, HALF, [[D, 128], [DH, NH], [1, HALF]])
        cos3 = bass.AP(cos_bf, c * HALF, [[ntab, 128], [0, NH], [1, HALF]])
        sin3 = bass.AP(sin_bf, c * HALF, [[ntab, 128], [0, NH], [1, HALF]])
        pt_lo = [f"pt:{k}:{ti}" for k in range(4)]
        pt_hi = [f"pt:{k}:{ti}" for k in range(4, NK)]
        dve(lambda v: v.tensor_tensor(rsa, lo, cosap, ALU.mult),
            reads=[srcb, "cos"], writes=pt_lo)
        dve(lambda v: v.tensor_tensor(rsb, hi, sinap, ALU.mult),
            reads=[srcb, "sin"], writes=pt_hi)
        dve(lambda v: v.tensor_tensor(rsc, lo3, sin3, ALU.mult),
            reads=[srcb, "sin"], writes=["kw"])
        dve(lambda v: v.tensor_tensor(rsd, hi3, cos3, ALU.mult),
            reads=[srcb, "cos"], writes=["kw"])
        dve(lambda v: v.tensor_tensor(lo, rsa, rsb, ALU.subtract),
            reads=pt_lo + pt_hi, writes=[srcb])
        dve(lambda v: v.tensor_tensor(hi3, rsc, rsd, ALU.add),
            reads=["kw"], writes=[srcb])

    def rope_q(tile):
        _rope_one(tile, q_tm, "q_tm")

    def rope_k_core(tile):
        _rope_one(tile, k_tm, "k_tm")

    def issue_tq(tile):
        for h in range(NH):
            r6 = h
            qd = sdma if h % 2 == 0 else adma
            qd(lambda e, h=h, r6=r6: e.dma_start(
                qofm[:, r6 * 128:(r6 + 1) * 128],
                q_tm[:, h * DH:(h + 1) * DH], transpose=True),
                reads=["q_tm"], writes=[f"qofm:{r6}"], tag=f"Tq:{h}")

    def issue_tk(tile):
        for h in range(NH):
            r6 = h
            qd = sdma if h % 2 == 1 else adma
            qd(lambda e, h=h, r6=r6: e.dma_start(
                kfm[:, r6 * 128:(r6 + 1) * 128],
                k_tm[:, h * DH:(h + 1) * DH], transpose=True),
                reads=["k_tm"], writes=[f"kfm:{r6}"], tag=f"Tk:{h}")

    def kw_fold(tile):
        wb = bass.AP(w_tm, 0, [[8, 128], [1, NH], [0, DH]])
        dve(lambda v: v.tensor_tensor(
            kw_tm[:].rearrange("p (h e) -> p h e", e=DH),
            k_tm[:].rearrange("p (h e) -> p h e", e=DH), wb, ALU.mult),
            reads=["k_tm", "w_tm"], writes=["kw"])

    def _sb_slots(h):
        r = h % 4
        r6 = h % 6
        return r, r6, BK[2 + r6], PB[2 + r6]

    def scan_mm(tile, h):
        r, r6, bs, pbs = _sb_slots(h)
        pe(lambda t: (
            t.matmul(pbs[:, 0:128], negsel[:, h * 128:(h + 1) * 128],
                     pg_hi[:], start=True, stop=False),
            t.matmul(pbs[:, 0:128], negsel[:, h * 128:(h + 1) * 128],
                     pg_lo[:], start=False, stop=True))[-1],
            reads=["negsel", "pghi", "pglo"], writes=[bs], tag="Lbc")

    def scan_s(tile, h):
        r, r6, bs, pbs = _sb_slots(h)
        pe(lambda t: t.matmul(pbs[:, 128:256], kfm[:, h * 128:(h + 1) * 128],
                              qofm[:, h * 128:(h + 1) * 128],
                              start=True, stop=True),
           reads=[f"kfm:{h}", f"qofm:{h}"], writes=[bs], tag="S")

    def scan_mneg(tile, h):
        r, r6, bs, pbs = _sb_slots(h)
        act(lambda s: s.activation(dtile[:, r * 128:(r + 1) * 128],
                                   pbs[:, 0:128], AF.Exp),
            reads=[bs], writes=[f"dtile:{r}"])
        dve(lambda v: v.tensor_tensor(pbs[:, 0:128], pbs[:, 0:128],
                                      mneg[:], ALU.add),
            reads=[bs, "mneg"], writes=[bs])

    def scan_exp(tile, h):
        r, r6, bs, pbs = _sb_slots(h)
        seq = tile // tps
        mslot = seq * NH + h
        act(lambda s: s.activation(atile[:, r * 128:(r + 1) * 128],
                                   pbs[:, 0:128], AF.Exp,
                                   bias=b2p[:, h:h + 1]),
            reads=[bs, "b2p"], writes=[f"atile:{r}"])
        gop(lambda g: g.tensor_copy(mbf[:, r * 128:(r + 1) * 128],
                                    msb[:, mslot * 128:(mslot + 1) * 128]),
            reads=[f"M:{mslot}"], writes=[f"mbf:{r}"])

    def scan_shat(tile, h):
        r, r6, bs, pbs = _sb_slots(h)
        dve(lambda v: v.tensor_tensor(shat[:, r * 128:(r + 1) * 128],
                                      pbs[:, 128:256],
                                      atile[:, r * 128:(r + 1) * 128],
                                      ALU.mult),
            reads=[bs, f"atile:{r}"], writes=[f"shat:{r}"])
        dve(lambda v: v.tensor_tensor(qhfm[:, r * 128:(r + 1) * 128],
                                      qofm[:, h * 128:(h + 1) * 128],
                                      dtile[:, r * 128:(r + 1) * 128],
                                      ALU.mult),
            reads=[f"qofm:{h}", f"dtile:{r}"], writes=[f"qhfm:{r}"])

    def scan_y(tile, h):
        r, r6, bs, pbs = _sb_slots(h)
        pe(lambda t: (
            t.matmul(pbs[:, 256:384],
                     v_tm[:, h * DH:(h + 1) * DH],
                     shat[:, r * 128:(r + 1) * 128], start=True, stop=False),
            t.matmul(pbs[:, 256:384], mbf[:, r * 128:(r + 1) * 128],
                     qhfm[:, r * 128:(r + 1) * 128],
                     start=False, stop=True))[-1],
            reads=["v_tm", f"shat:{r}", f"mbf:{r}", f"qhfm:{r}"],
            writes=[bs], tag="Y")
        act(lambda s: s.activation(yfm[:, h * 128:(h + 1) * 128],
                                   pbs[:, 256:384], AF.Copy),
            reads=[bs], writes=[f"y:{h}"])

    def scan_kv(tile, h):
        r, r6, bs, pbs = _sb_slots(h)
        seq = tile // tps
        mslot = seq * NH + h
        msl = msb[:, mslot * 128:(mslot + 1) * 128]
        pe(lambda t: t.matmul(pbs[:, 384:512], kw_tm[:, h * DH:(h + 1) * DH],
                              v_tm[:, h * DH:(h + 1) * DH],
                              start=True, stop=True),
           reads=["kw", "v_tm"], writes=[bs], tag="KV")
        dve(lambda v: v.scalar_tensor_tensor(
            msl, msl, dc_t[:, h:h + 1], pbs[:, 384:512], ALU.mult, ALU.add),
            reads=[f"M:{mslot}", "dc", bs], writes=[f"M:{mslot}"], tag="Mupd")

    def outproj(tile):
        par = tile % 2
        ti = tile % GTILES
        yreads = [f"y:{h}" for h in range(NH)]
        toff = ti * TILE
        for ec in range(NK):
            bk = 2 + ec % 4
            off = 128 * (ec // 4)

            def fn(t, ec=ec, bk=bk, off=off):
                last = None
                for h in range(NH):
                    last = t.matmul(
                        PB[bk][:, off:off + 128],
                        wo[:, h * D + ec * 128:h * D + (ec + 1) * 128],
                        yfm[:, h * 128:(h + 1) * 128],
                        start=(h == 0), stop=(h == NH - 1))
                return last
            pe(fn, reads=["wo"] + yreads, writes=[BK[bk]], tag="outproj")
            dve(lambda v, ec=ec, bk=bk, off=off, toff=toff:
                v.scalar_tensor_tensor(
                    partial[:, ec * GTOK + toff:ec * GTOK + toff + TILE],
                    PB[bk][:, off:off + 128], bout_c[:, ec:ec + 1],
                    xf_st[:, par * D + ec * 128:par * D + (ec + 1) * 128],
                    ALU.add, ALU.add),
                reads=[BK[bk], "bout_c", f"xf:{par}"],
                writes=[f"pt:{ec}:{ti}"], tag="partial")

    def ffn_group(gi):
        xn_reads = [f"xn:{k}:{tt}" for k in range(NK) for tt in range(GTILES)]
        for c2 in range(NC2):
            wpar = c2 % 2
            gdma(lambda g, c2=c2, wpar=wpar: g.dma_start(
                wg_win[:, wpar * D:(wpar + 1) * D].rearrange(
                    "p (k w) -> p k w", k=NK),
                bass.AP(dr["wg_c"].tensor, c2 * 128,
                        [[H2, 128], [H2 * 128, NK], [1, 128]])),
                reads=["wg_c:0", "wg_c:1"],
                writes=[f"wgw:{wpar}"], tag=f"wg:{c2}")
            sdma(lambda g, c2=c2, wpar=wpar: g.dma_start(
                wu_win[:, wpar * D:(wpar + 1) * D].rearrange(
                    "p (k w) -> p k w", k=NK),
                bass.AP(dr["wu_c"].tensor, c2 * 128,
                        [[H2, 128], [H2 * 128, NK], [1, 128]])),
                reads=["wu_c:0", "wu_c:1"],
                writes=[f"wuw:{wpar}"], tag=f"wu:{c2}")
            gb = (0, 1) if c2 % 2 == 0 else (4, 5)
            ub = (2, 3) if c2 % 2 == 0 else (6, 7)

            def fgate(t, wpar=wpar, gb=gb):
                last = None
                for k in range(NK):
                    w = wg_win[:, wpar * D + k * 128:wpar * D + (k + 1) * 128]
                    last = t.matmul(PB[gb[0]][:, 0:512], w,
                                    xn_fm[:, k * GTOK:k * GTOK + 512],
                                    start=(k == 0), stop=(k == NK - 1))
                    last = t.matmul(PB[gb[1]][:, 0:512], w,
                                    xn_fm[:, k * GTOK + 512:(k + 1) * GTOK],
                                    start=(k == 0), stop=(k == NK - 1))
                return last
            pe(fgate, reads=[f"wgw:{wpar}"] + xn_reads,
               writes=[BK[gb[0]], BK[gb[1]]], tag="ffn_gate")

            def fup(t, wpar=wpar, ub=ub):
                last = None
                for k in range(NK):
                    w = wu_win[:, wpar * D + k * 128:wpar * D + (k + 1) * 128]
                    last = t.matmul(PB[ub[0]][:, 0:512], w,
                                    xn_fm[:, k * GTOK:k * GTOK + 512],
                                    start=(k == 0), stop=(k == NK - 1))
                    last = t.matmul(PB[ub[1]][:, 0:512], w,
                                    xn_fm[:, k * GTOK + 512:(k + 1) * GTOK],
                                    start=(k == 0), stop=(k == NK - 1))
                return last
            pe(fup, reads=[f"wuw:{wpar}"] + xn_reads,
               writes=[BK[ub[0]], BK[ub[1]]], tag="ffn_up")
            for half in range(2):
                act(lambda s, c2=c2, gb=gb, half=half: s.activation(
                    hh[:, c2 * GTOK + half * 512:c2 * GTOK + (half + 1) * 512],
                    PB[gb[half]][:, 0:512], AF.Silu, scale=1.0 / 127.0),
                    reads=[BK[gb[half]]], writes=[f"hh:{c2}"], tag="silu")
                dve(lambda v, c2=c2, ub=ub, half=half: v.scalar_tensor_tensor(
                    hh[:, c2 * GTOK + half * 512:c2 * GTOK + (half + 1) * 512],
                    PB[ub[half]][:, 0:512], 1.0 / 127.0,
                    hh[:, c2 * GTOK + half * 512:c2 * GTOK + (half + 1) * 512],
                    ALU.mult, ALU.mult),
                    reads=[BK[ub[half]], f"hh:{c2}"], writes=[f"hh:{c2}"],
                    tag="hhmul")
        seq = (gi * GTILES) // tps
        goff = ((gi * GTILES) % tps) * TILE
        for p in range(2):
            AH = 3

            def _wd_dma(c2, p=p):
                wpar = c2 % 4
                gdma(lambda g, c2=c2, p=p, wpar=wpar: g.dma_start(
                    wd_win[:, wpar * 512:(wpar + 1) * 512],
                    bass.AP(dr["wd_c"].tensor, c2 * 128 * D + p * 512,
                            [[D, 128], [1, 512]])),
                    reads=["wd_c:0", "wd_c:1"],
                    writes=[f"wdw:{wpar}"], tag=f"wd:{p}:{c2}")

            def _wd_mm(c2, p=p):
                wpar = c2 % 4

                def fdown(t, c2=c2, wpar=wpar):
                    last = None
                    for dc in range(4):
                        w = wd_win[:, wpar * 512 + dc * 128:
                                   wpar * 512 + (dc + 1) * 128]
                        last = t.matmul(PB[2 * dc][:, 0:512], w,
                                        hh[:, c2 * GTOK:c2 * GTOK + 512],
                                        start=(c2 == 0), stop=(c2 == NC2 - 1))
                        last = t.matmul(PB[2 * dc + 1][:, 0:512], w,
                                        hh[:, c2 * GTOK + 512:(c2 + 1) * GTOK],
                                        start=(c2 == 0), stop=(c2 == NC2 - 1))
                    return last
                pe(fdown, reads=[f"wdw:{wpar}", f"hh:{c2}"],
                   writes=[BK[i] for i in range(8)], tag="down")

            for c2 in range(NC2 + AH):
                if c2 < NC2:
                    _wd_dma(c2)
                if c2 >= AH:
                    _wd_mm(c2 - AH)
            for dc in range(4):
                ec = p * 4 + dc
                pt_reads = [f"pt:{ec}:{tt}" for tt in range(GTILES)]
                for qr in range(4):
                    sl_o = out_sb[:, (qr % 2) * 256:(qr % 2 + 1) * 256]
                    dve(lambda v, dc=dc, ec=ec, qr=qr, sl_o=sl_o:
                        v.scalar_tensor_tensor(
                            sl_o,
                            PB[2 * dc + qr // 2][:, (qr % 2) * 256:
                                                 (qr % 2 + 1) * 256],
                            1.0 / 127.0,
                            partial[:, ec * GTOK + qr * 256:
                                    ec * GTOK + (qr + 1) * 256],
                            ALU.mult, ALU.add),
                        reads=[BK[2 * dc + qr // 2]] + pt_reads,
                        writes=[f"out_sb:{qr % 2}"], tag="outevac")

                    def fst(sy, ec=ec, seq=seq, goff=goff, qr=qr, sl_o=sl_o):
                        dst = bass.AP(dr["out_fm"].tensor,
                                      (seq * D + ec * 128) * l_seq + goff +
                                      qr * 256,
                                      [[l_seq, 128], [1, 256]])
                        return sy.dma_start(dst, sl_o)
                    sdma(fst, reads=[f"out_sb:{qr % 2}"],
                         tag=f"store:{gi}:{ec}:{qr}")

    loaded = [False] * ntile

    def ensure_load(t):
        if 0 <= t < ntile and not loaded[t]:
            loaded[t] = True
            x_load(t)

    ensure_load(0)
    ensure_load(1)
    ln_stage(0)
    for tile in range(ntile):
        nxt = tile + 1
        ensure_load(nxt)
        pass_a_qz(tile)
        evac_q(tile)
        gates_pre(tile)
        rope_q(tile)
        issue_tq(tile)
        pass_a_k(tile)
        evac_k(tile)
        gates_mms_a(tile)
        if nxt < ntile and nxt % GTILES != 0:
            ln_square(nxt)
        rope_k_core(tile)
        issue_tk(tile)
        pass_a_v(tile)
        gates_mms_b(tile)
        kw_fold(tile)
        evac_v(tile)
        thunks = ln_thunks(nxt) if (nxt < ntile and nxt % GTILES != 0) else []
        nslots = NH + 5
        npc = (len(thunks) + nslots - 1) // nslots if thunks else 0
        idx = 0
        for s in range(nslots):
            if s < NH:
                scan_mm(tile, s)
            if 0 <= s - 1 < NH:
                scan_mneg(tile, s - 1)
            if 0 <= s - 2 < NH:
                scan_exp(tile, s - 2)
            if 0 <= s - 3 < NH:
                scan_shat(tile, s - 3)
            if 0 <= s - 4 < NH:
                scan_y(tile, s - 4)
            if 0 <= s - 5 < NH:
                scan_kv(tile, s - 5)
            for _ in range(npc):
                if idx < len(thunks):
                    thunks[idx]()
                    idx += 1
            if s < NH:
                scan_s(tile, s)
        while idx < len(thunks):
            thunks[idx]()
            idx += 1
        outproj(tile)
        while idx < len(thunks):
            thunks[idx]()
            idx += 1
        if nxt % GTILES == 0:
            ffn_group(tile // GTILES)
            if nxt < ntile:
                ln_stage(nxt)

    with nc.Block() as block:
        def emit(eng_obj, lst):
            for (fn, waits, sem, amt, tag) in lst:
                for (s, v) in waits:
                    eng_obj.wait_ge(s, v)
                ins = fn(eng_obj)
                ins.then_inc(sem, amt)

        @block.tensor
        def _(t):
            emit(t, P.q["pe"])

        @block.vector
        def _(v):
            emit(v, P.q["dve"])

        @block.scalar
        def _(s):
            emit(s, P.q["act"])

        @block.gpsimd
        def _(g):
            emit(g, P.q["gps"])

        @block.sync
        def _(sy):
            emit(sy, P.q["sp"])
            for (s, v) in P.final_waits("sp"):
                sy.wait_ge(s, v)

    return nc


def _layout_inputs(inputs, n_seq, l_seq, n_cores):
    f = np.ascontiguousarray
    x = np.asarray(inputs["L0_attn_x"], dtype=np.float32)
    base = {
        "wq_t": f(np.asarray(inputs["L0_attn_Wq"], np.float32).T),
        "wk_t": f(np.asarray(inputs["L0_attn_Wk"], np.float32).T),
        "wv_t": f(np.asarray(inputs["L0_attn_Wv"], np.float32).T),
        "wout_t": f(np.asarray(inputs["L0_attn_Wout"], np.float32).T),
        "wg_all": f(np.concatenate(
            [np.asarray(inputs["L0_attn_Wbeta"], np.float32).T,
             np.asarray(inputs["L0_attn_Wig"], np.float32).T,
             np.asarray(inputs["L0_attn_Wog"], np.float32).T], axis=1)),
        "b24": f(np.concatenate(
            [np.asarray(inputs["L0_attn_bbeta"], np.float32),
             np.asarray(inputs["L0_attn_big"], np.float32),
             np.asarray(inputs["L0_attn_bog"], np.float32)])[None, :]),
        "bout": f(np.asarray(inputs["L0_attn_bout"], np.float32)[None, :]),
        "nw": f(np.asarray(inputs["L0_attn_norm_w"], np.float32)[None, :]),
        "wgate_t": f(np.asarray(inputs["L0_ffn_w_gate"], np.int32).T),
        "wup_t": f(np.asarray(inputs["L0_ffn_w_up"], np.int32).T),
        "wdown_t": f(np.asarray(inputs["L0_ffn_w_down"], np.int32).T),
    }
    in_maps = []
    for c in range(n_cores):
        xs = x[c * n_seq:(c + 1) * n_seq, :l_seq, :]
        m = dict(base)
        m["x_fm"] = f(xs.transpose(0, 2, 1)).reshape(n_seq * D, l_seq)
        in_maps.append(m)
    return in_maps


def kernel(**inputs):
    n_seq, l_seq, n_cores = 2, 4096, 8
    in_maps = _layout_inputs(inputs, n_seq, l_seq, n_cores)
    nc = build_program(n_seq=n_seq, l_seq=l_seq)
    res = run_bass_kernel_spmd(nc, in_maps, core_ids=list(range(n_cores)))
    outs = []
    for c in range(n_cores):
        o = np.asarray(res.results[c]["out_fm"], dtype=np.float32)
        o = o.reshape(n_seq, D, l_seq).transpose(0, 2, 1)
        outs.append(o)
    return np.ascontiguousarray(np.concatenate(outs, axis=0))


# version 28 (final)
# speedup vs baseline: 1.0621x; 1.0123x over previous
import math
import numpy as np
import concourse.bass as bass
import concourse.mybir as mybir
from concourse.bass_utils import run_bass_kernel_spmd

F32 = mybir.dt.float32
BF16 = mybir.dt.bfloat16
I32 = mybir.dt.int32
AF = mybir.ActivationFunctionType
ALU = mybir.AluOpType

D = 1024
NH = 8
DH = 128
HALF = 64
H2 = 3072
NC2 = H2 // 128
NK = D // 128
TILE = 128
GTILES = 8
GTOK = TILE * GTILES
SCALE = 1.0 / math.sqrt(DH)
LNSCALE = math.log(SCALE)
THETA = 10000.0
TWO_PI = 2.0 * math.pi
_c1 = np.float32(6.28125)
_c2r = np.float64(TWO_PI) - np.float64(_c1)
_c2 = np.frombuffer((np.frombuffer(np.float32(_c2r).tobytes(),
                                   dtype=np.uint32) &
                     np.uint32(0xFFFFF000)).tobytes(), dtype=np.float32)[0]
_c3 = np.float32(np.float64(TWO_PI) - np.float64(_c1) - np.float64(_c2))
C1, C2, C3 = float(_c1), float(_c2), float(_c3)


class _Prog:

    ENGINES = ("pe", "dve", "act", "gps", "sp")

    def __init__(self, nc):
        self.nc = nc
        self.q = {e: [] for e in self.ENGINES}
        self.sem = {}
        for name in ("s_pe", "s_dve", "s_act", "s_gpc", "s_sp"):
            self.sem[name] = [nc.alloc_semaphore(name), 0]
        self.esem = {"pe": "s_pe", "dve": "s_dve", "act": "s_act",
                     "gps": "s_gpc", "sp": "s_sp"}
        self.dring = {}
        for qn, n in (("gps", 10), ("sp", 8), ("act", 6)):
            for j in range(n):
                nm = f"s_d{qn}{j}"
                self.sem[nm] = [nc.alloc_semaphore(nm), 0]
            self.dring[qn] = (n, [0])
        self.buf = {}
        self.clock = {e: {} for e in self.ENGINES}
        self.barrier_tok = None
        self.barrier_pending = set()

    def op(self, eng, fn, reads=(), writes=(), dma=False, tag=""):
        waits = {}
        if dma:
            nring, cnt = self.dring[eng if eng != "actq" else "act"]
            j = cnt[0] % nring
            cnt[0] += 1
            sname = f"s_d{eng if eng != 'actq' else 'act'}{j}"
            amt = 16
            if self.sem[sname][1] > 0:
                waits[sname] = self.sem[sname][1]
        else:
            sname = self.esem[eng]
            amt = 1
        if eng == "actq":
            eng = "act"
        if eng in self.barrier_pending:
            self.barrier_pending.discard(eng)
            for s, v in self.barrier_tok.items():
                waits[s] = max(waits.get(s, 0), v)
        for b in reads:
            e = self.buf.get(b)
            if e and e["w"] is not None:
                s, v = e["w"]
                waits[s] = max(waits.get(s, 0), v)
        for b in writes:
            e = self.buf.get(b)
            if e:
                if e["w"] is not None:
                    s, v = e["w"]
                    waits[s] = max(waits.get(s, 0), v)
                for (s, v) in e["r"]:
                    waits[s] = max(waits.get(s, 0), v)
        ck = self.clock[eng]
        wl = [(s, v) for (s, v) in waits.items() if ck.get(s, -1) < v]
        for (s, v) in wl:
            ck[s] = v
        self.sem[sname][1] += amt
        tok = (sname, self.sem[sname][1])
        self.q[eng].append((fn, [(self.sem[s][0], v) for (s, v) in wl],
                            self.sem[sname][0], amt, tag))
        for b in reads:
            self.buf.setdefault(b, {"w": None, "r": []})["r"].append(tok)
        for b in writes:
            self.buf[b] = {"w": tok, "r": []}
        return tok

    def barrier(self):
        tot = {}
        for e in self.buf.values():
            for tok in ([e["w"]] if e["w"] else []) + e["r"]:
                s, v = tok
                tot[s] = max(tot.get(s, 0), v)
        self.barrier_tok = tot
        self.barrier_pending = set(self.ENGINES)

    def final_waits(self, eng):
        out = {}
        for e in self.buf.values():
            for tok in ([e["w"]] if e["w"] else []) + e["r"]:
                s, v = tok
                out[s] = max(out.get(s, 0), v)
        ck = self.clock[eng]
        return [(self.sem[s][0], v) for s, v in out.items()
                if ck.get(s, -1) < v]


def build_program(n_seq=2, l_seq=4096):
    ntile = n_seq * l_seq // TILE
    ngroup = ntile // GTILES
    tps = l_seq // TILE
    nct = tps
    ntab = nct * HALF
    nc = bass.Bass("TRN2", target_bir_lowering=False, debug=False)

    dr = {}
    dr["x_fm"] = nc.dram_tensor("x_fm", [n_seq * D, l_seq], F32,
                                kind="ExternalInput").ap()
    for nm in ("wq_t", "wk_t", "wv_t", "wout_t"):
        dr[nm] = nc.dram_tensor(nm, [D, D], F32, kind="ExternalInput").ap()
    dr["wg_all"] = nc.dram_tensor("wg_all", [D, 24], F32,
                                  kind="ExternalInput").ap()
    dr["b24"] = nc.dram_tensor("b24", [1, 24], F32, kind="ExternalInput").ap()
    dr["bout"] = nc.dram_tensor("bout", [1, D], F32, kind="ExternalInput").ap()
    dr["nw"] = nc.dram_tensor("nw", [1, D], F32, kind="ExternalInput").ap()
    dr["wgate_t"] = nc.dram_tensor("wgate_t", [D, H2], I32,
                                   kind="ExternalInput").ap()
    dr["wup_t"] = nc.dram_tensor("wup_t", [D, H2], I32,
                                 kind="ExternalInput").ap()
    dr["wdown_t"] = nc.dram_tensor("wdown_t", [H2, D], I32,
                                   kind="ExternalInput").ap()
    dr["out_fm"] = nc.dram_tensor("out_fm", [n_seq * D, l_seq], F32,
                                  kind="ExternalOutput").ap()
    dr["wg_c"] = nc.dram_tensor("wg_c", [D, H2], BF16, kind="Internal").ap()
    dr["wu_c"] = nc.dram_tensor("wu_c", [D, H2], BF16, kind="Internal").ap()
    dr["wd_c"] = nc.dram_tensor("wd_c", [H2, D], BF16, kind="Internal").ap()

    PB = [nc.alloc_psum_tensor(f"PB{i}", [128, 512], F32) for i in range(8)]
    BK = [f"@B{i}" for i in range(8)]

    P = _Prog(nc)
    gop = lambda fn, reads=(), writes=(), tag="": P.op(
        "gps", fn, reads, writes, tag=tag)
    gdma = lambda fn, reads=(), writes=(), tag="": P.op(
        "gps", fn, reads, writes, dma=True, tag=tag)
    sdma = lambda fn, reads=(), writes=(), tag="": P.op(
        "sp", fn, reads, writes, dma=True, tag=tag)
    adma = lambda fn, reads=(), writes=(), tag="": P.op(
        "actq", fn, reads, writes, dma=True, tag=tag)
    pe = lambda fn, reads=(), writes=(), tag="": P.op(
        "pe", fn, reads, writes, tag=tag)
    dve = lambda fn, reads=(), writes=(), tag="": P.op(
        "dve", fn, reads, writes, tag=tag)
    act = lambda fn, reads=(), writes=(), tag="": P.op(
        "act", fn, reads, writes, tag=tag)

    sbal = nc.alloc_sbuf_tensor

    cos_bf = sbal("cos_bf", [128, ntab], BF16)
    sin_bf = sbal("sin_bf", [128, ntab], BF16)
    tri_bf = sbal("tri_bf", [128, 128], BF16)
    mneg = sbal("mneg", [128, 128], BF16)
    ident = sbal("ident", [128, 128], BF16)
    ident_f = sbal("ident_f", [1, 1], F32)
    negsel = sbal("negsel", [8, NH * 128], BF16)
    ones_c = sbal("ones_c", [128, 1], BF16)
    ones_r = sbal("ones_r", [1, 128], BF16)
    nones_r = sbal("nones_r", [1, 128], BF16)
    cp1 = sbal("cp1", [128, 1], F32)
    cpeps = sbal("cpeps", [1, 2], F32)
    nw_col = sbal("nw_col", [128, NK], F32)
    bout_c = sbal("bout_c", [128, NK], F32)
    b24_t = sbal("b24_t", [128, 24], F32)
    b24_bf = sbal("b24_bf", [1, 24], BF16)

    gop(lambda g: g.memset(tri_bf[:], 1.0), writes=["tri*"])
    gop(lambda g: g.affine_select(tri_bf[:], tri_bf[:], [[1, 128]],
                                  ALU.is_ge, 0.0, base=0,
                                  channel_multiplier=-1),
        reads=["tri*"], writes=["tri"])
    gop(lambda g: g.memset(mneg[:], 0.0), writes=["mneg*"])
    gop(lambda g: g.affine_select(mneg[:], mneg[:], [[1, 128]],
                                  ALU.is_ge, -1.0e30, base=0,
                                  channel_multiplier=-1),
        reads=["mneg*"], writes=["mneg"])
    gop(lambda g: g.memset(ident[:], 1.0), writes=["ident*"])
    gop(lambda g: g.affine_select(ident[:], ident[:], [[1, 128]],
                                  ALU.is_equal, 0.0, base=0,
                                  channel_multiplier=-1),
        reads=["ident*"], writes=["ident"])
    gop(lambda g: g.memset(ident_f[:], 1.0), writes=["identf"])
    gop(lambda g: g.memset(negsel[:], -1.0), writes=["negsel*"])
    gop(lambda g: g.affine_select(negsel[:], negsel[:], [[1, NH], [0, 128]],
                                  ALU.is_equal, 0.0, base=0,
                                  channel_multiplier=-1),
        reads=["negsel*"], writes=["negsel"])
    gop(lambda g: g.memset(ones_c[:], 1.0), writes=["ones_c"])
    gop(lambda g: g.memset(ones_r[:], 1.0), writes=["ones_r"])
    gop(lambda g: g.memset(nones_r[:], -1.0), writes=["nones_r"])
    gop(lambda g: g.memset(cp1[:], 1.0), writes=["cp1"])
    gop(lambda g: g.memset(cphp[:], math.pi / 2.0), writes=["cphp"])
    gop(lambda g: g.memset(cpeps[:, 0:1], 1.0e-5), writes=["cpeps"])

    gdma(lambda g: g.dma_start(
        nw_col[:], bass.AP(dr["nw"].tensor, 0, [[1, 128], [128, NK]]),
        allow_slow_non_contiguous=True),
        writes=["nw_col"])
    gdma(lambda g: g.dma_start(
        bout_c[:], bass.AP(dr["bout"].tensor, 0, [[1, 128], [128, NK]]),
        allow_slow_non_contiguous=True),
        writes=["bout_c"])
    gdma(lambda g: g.dma_start(b24_bf[:], dr["b24"]), writes=["b24bf"])
    pe(lambda t: t.matmul(PB[6][:, 0:24], ones_r[:], b24_bf[:],
                          start=True, stop=True),
       reads=["ones_r", "b24bf"], writes=[BK[6]], tag="b24bc")
    dve(lambda v: v.tensor_copy(b24_t[:], PB[6][:, 0:24]),
        reads=[BK[6]], writes=["b24"])

    wq = sbal("wq", [128, NK * D], BF16)
    wk = sbal("wk", [128, NK * D], BF16)
    wv = sbal("wv", [128, NK * D], BF16)
    wgs = sbal("wgs", [128, NK * 24], BF16)
    wo = sbal("wo", [128, NK * D], BF16)

    def wload(dst, src_name, width, bufname):
        def fn(g, dst=dst, src_name=src_name, width=width):
            src = bass.AP(dr[src_name].tensor, 0,
                          [[width, 128], [width * 128, NK], [1, width]])
            return g.dma_start(
                dst[:].rearrange("p (k w) -> p k w", k=NK), src)
        gdma(fn, writes=[bufname], tag=f"load:{bufname}")

    wload(wq, "wq_t", D, "wq")
    wload(wk, "wk_t", D, "wk")
    wload(wv, "wv_t", D, "wv")
    wload(wgs, "wg_all", 24, "wgs")
    wload(wo, "wout_t", D, "wo")
    for (cn, sn, rows) in (("wg_c", "wgate_t", D), ("wu_c", "wup_t", D),
                           ("wd_c", "wdown_t", H2)):
        for hhf in range(2):
            gdma(lambda g, cn=cn, sn=sn, rows=rows, hhf=hhf: g.dma_start(
                dr[cn].rearrange("a b -> a b")[hhf * (rows // 2):
                                               (hhf + 1) * (rows // 2), :],
                dr[sn].rearrange("a b -> a b")[hhf * (rows // 2):
                                               (hhf + 1) * (rows // 2), :]),
                writes=[f"{cn}:{hhf}"], tag=f"cache:{cn}:{hhf}")


    from contextlib import ExitStack
    _scratch = ExitStack()

    def scr(name, shape, dtype):
        return _scratch.enter_context(nc.sbuf_tensor(name, shape, dtype))

    tb_f = scr("tb_f", [128, ntab], F32)
    tb_g = scr("tb_g", [128, ntab], F32)
    tb_h = scr("tb_h", [128, ntab], F32)
    tb_i = scr("tb_i", [128, ntab], I32)
    r0f = scr("r0f", [1, ntab], F32)
    r1f = scr("r1f", [1, ntab], F32)
    r0h = scr("r0h", [1, ntab], BF16)
    r0l = scr("r0l", [1, ntab], BF16)
    r1h = scr("r1h", [1, ntab], BF16)
    r1l = scr("r1l", [1, ntab], BF16)
    tvals = scr("tvals", [1, 128], BF16)
    invf = scr("invf", [1, HALF], F32)
    cphp = scr("cphp", [128, 1], F32)
    t_iot = scr("t_iot", [1, 128], F32)

    gop(lambda g: g.iota(t_iot[:], [[1, 128]], base=0, channel_multiplier=0,
                         allow_small_or_imprecise_dtypes=True),
        writes=["iot"])
    act(lambda s: s.activation(invf[:], t_iot[0:1, 0:HALF], AF.Exp,
                               scale=-math.log(THETA) / HALF),
        reads=["iot"], writes=["invf"])
    dve(lambda v: v.tensor_copy(tvals[:], t_iot[:]),
        reads=["iot"], writes=["tvals"])
    gop(lambda g: g.iota(r0f[:], [[1, nct], [0, HALF]], base=0,
                         channel_multiplier=0,
                         allow_small_or_imprecise_dtypes=True),
        writes=["r0f"])

    def _rhs0(v):
        invb = bass.AP(invf, 0, [[HALF, 1], [0, nct], [1, HALF]])
        return v.tensor_tensor(
            r0f[:].rearrange("o (c i) -> o c i", i=HALF),
            r0f[:].rearrange("o (c i) -> o c i", i=HALF),
            invb, ALU.mult)
    dve(_rhs0, reads=["r0f", "invf"], writes=["r0f"])
    dve(lambda v: v.tensor_scalar_mul(r0f[:], r0f[:], 128.0),
        reads=["r0f"], writes=["r0f"])

    def _rhs1(v):
        invb = bass.AP(invf, 0, [[HALF, 1], [0, nct], [1, HALF]])
        return v.tensor_copy(r1f[:].rearrange("o (c i) -> o c i", i=HALF),
                             invb)
    dve(_rhs1, reads=["invf"], writes=["r1f"])
    dve(lambda v: v.tensor_copy(r0h[:], r0f[:]), reads=["r0f"],
        writes=["r0h"])
    dve(lambda v: v.tensor_tensor(r0l[:], r0f[:], r0h[:], ALU.subtract),
        reads=["r0f", "r0h"], writes=["r0l"])
    dve(lambda v: v.tensor_copy(r1h[:], r1f[:]), reads=["r1f"],
        writes=["r1h"])
    dve(lambda v: v.tensor_tensor(r1l[:], r1f[:], r1h[:], ALU.subtract),
        reads=["r1f", "r1h"], writes=["r1l"])
    nq = min(ntab, 512)
    nmm = (ntab + nq - 1) // nq
    for i in range(nmm):
        bk = i % 4
        sl_ = slice(i * nq, (i + 1) * nq)

        def fmm(t, sl_=sl_, bk=bk):
            t.matmul(PB[bk][:, 0:nq], ones_r[:], r0h[0:1, sl_],
                     start=True, stop=False)
            t.matmul(PB[bk][:, 0:nq], ones_r[:], r0l[0:1, sl_],
                     start=False, stop=False)
            t.matmul(PB[bk][:, 0:nq], tvals[:], r1h[0:1, sl_],
                     start=False, stop=False)
            return t.matmul(PB[bk][:, 0:nq], tvals[:], r1l[0:1, sl_],
                            start=False, stop=True)
        pe(fmm, reads=["ones_r", "tvals", "r0h", "r0l", "r1h", "r1l"],
           writes=[BK[bk]], tag="freqmm")
        dve(lambda v, sl_=sl_, bk=bk: v.tensor_copy(tb_f[:, sl_],
                                                    PB[bk][:, 0:nq]),
            reads=[BK[bk]], writes=["tb_f"])
    dve(lambda v: v.tensor_scalar(tb_g[:], tb_f[:], 1.0 / TWO_PI, 0.5,
                                  ALU.mult, ALU.add),
        reads=["tb_f"], writes=["tb_g"])
    dve(lambda v: v.tensor_copy(tb_i[:], tb_g[:]),
        reads=["tb_g"], writes=["tb_i"])
    dve(lambda v: v.tensor_copy(tb_g[:], tb_i[:]),
        reads=["tb_i"], writes=["tb_g"])
    for c in (C1, C2, C3):
        dve(lambda v, c=c: v.scalar_tensor_tensor(
            tb_f[:], tb_g[:], -c, tb_f[:], ALU.mult, ALU.add),
            reads=["tb_f", "tb_g"], writes=["tb_f"])
    dve(lambda v: v.tensor_scalar_min(tb_f[:], tb_f[:], math.pi),
        reads=["tb_f"], writes=["tb_f"])
    dve(lambda v: v.tensor_scalar_max(tb_f[:], tb_f[:], -math.pi),
        reads=["tb_f"], writes=["tb_f"])
    act(lambda s: s.activation(tb_h[:], tb_f[:], AF.Sin),
        reads=["tb_f"], writes=["tb_h"])
    dve(lambda v: v.tensor_copy(sin_bf[:], tb_h[:]),
        reads=["tb_h"], writes=["sin"])
    act(lambda s: s.activation(tb_g[:], tb_f[:], AF.Abs),
        reads=["tb_f"], writes=["tb_g"])
    act(lambda s: s.activation(tb_h[:], tb_g[:], AF.Sin, scale=-1.0,
                               bias=cphp[:, 0:1]),
        reads=["tb_g", "cphp"], writes=["tb_h"])
    dve(lambda v: v.tensor_copy(cos_bf[:], tb_h[:]),
        reads=["tb_h"], writes=["cos"])

    _scratch.close()

    wg_win = sbal("wg_win", [128, 2 * D], BF16)
    wu_win = sbal("wu_win", [128, 2 * D], BF16)
    wd_win = sbal("wd_win", [128, 4 * 512], BF16)
    xn_fm = sbal("xn_fm", [128, NK * GTOK], BF16)
    partial = sbal("partial", [128, NK * GTOK], BF16)
    hh = sbal("hh", [128, NC2 * GTOK], BF16)
    xf_st = sbal("xf_st", [128, 2 * D], BF16)
    out_sb = sbal("out_sb", [128, 2 * 256], F32)
    q_tm = sbal("q_tm", [128, D], BF16)
    k_tm = sbal("k_tm", [128, D], BF16)
    v_tm = sbal("v_tm", [128, D], BF16)
    kw_tm = sbal("kw_tm", [128, D], BF16)
    xnw_st = sbal("xnw_st", [128, NK * 128], BF16)
    kfm = sbal("kfm", [128, 8 * 128], BF16)
    qofm = sbal("qofm", [128, 8 * 128], BF16)
    qhfm = sbal("qhfm", [128, 4 * 128], BF16)
    shat = sbal("shat", [128, 4 * 128], BF16)
    atile = sbal("atile", [128, 4 * 128], BF16)
    dtile = sbal("dtile", [128, 4 * 128], BF16)
    mbf = sbal("mbf", [128, 4 * 128], BF16)
    yfm = sbal("yfm", [128, NH * 128], BF16)
    msb = sbal("msb", [128, n_seq * NH * 128], F32)
    zrow = sbal("zrow", [128, 24], F32)
    gsc = sbal("gsc", [128, 24], F32)
    spb = sbal("spb", [128, 16], F32)
    sp_hi = sbal("sp_hi", [128, 16], BF16)
    sp_lo = sbal("sp_lo", [128, 16], BF16)
    p_tm = sbal("p_tm", [128, 8], F32)
    pg_hi = sbal("pg_hi", [8, 128], BF16)
    pg_lo = sbal("pg_lo", [8, 128], BF16)
    pc_hi = sbal("pc_hi", [1, 8], BF16)
    pc_lo = sbal("pc_lo", [1, 8], BF16)
    b2p = sbal("b2p", [128, 8], F32)
    w_tm = sbal("w_tm", [128, 8], BF16)
    dc_t = sbal("dc_t", [128, 8], F32)
    wargt = sbal("wargt", [128, 8], F32)
    srow = sbal("srow", [1, 4 * 128], F32)
    srow_bf = sbal("srow_bf", [1, 2 * 128], BF16)
    acol = sbal("acol", [128, 1], F32)
    s_sb = sbal("s_sb", [128, 128], BF16)

    gop(lambda g: g.memset(msb[:], 0.0),
        writes=[f"M:{i}" for i in range(n_seq * NH)])

    P.barrier()

    def x_load(tile):
        par = tile % 2
        seq = tile // tps
        c = tile % tps

        def fn(g, par=par, seq=seq, c=c):
            src = bass.AP(dr["x_fm"].tensor, seq * D * l_seq + c * TILE,
                          [[l_seq, 128], [l_seq * 128, NK], [1, TILE]])
            return g.dma_start(
                xf_st[:, par * D:(par + 1) * D].rearrange(
                    "p (k t) -> p k t", k=NK), src)
        gdma(fn, writes=[f"xf:{par}"], tag=f"xload:{tile}")

    def R(i):
        return srow[0:1, i * 128:(i + 1) * 128]

    def ln_thunks(tile):
        TH = []

        def dpe(*a, **kw):
            TH.append(lambda a=a, kw=kw: pe(*a, **kw))

        def ddve(*a, **kw):
            TH.append(lambda a=a, kw=kw: dve(*a, **kw))

        def dact(*a, **kw):
            TH.append(lambda a=a, kw=kw: act(*a, **kw))
        _ln_body(tile, dpe, ddve, dact, square=False)
        return TH

    def ln_square(tile):
        par = tile % 2
        ti = tile % GTILES
        xf = xf_st[:, par * D:(par + 1) * D]
        toff0 = ti * TILE
        sq_names = [f"pt:{k}:{ti}" for k in range(NK)]
        sqfull = bass.AP(partial.tensor if hasattr(partial, "tensor")
                         else partial, toff0,
                         [[NK * GTOK, 128], [GTOK, NK], [1, TILE]])
        dve(lambda v: v.tensor_tensor(sqfull, xf[:].rearrange(
            "p (k t) -> p k t", k=NK), xf[:].rearrange(
            "p (k t) -> p k t", k=NK), ALU.mult),
            reads=[f"xf:{par}"], writes=sq_names)

    def ln_stage(tile):
        ln_square(tile)
        for th in ln_thunks(tile):
            th()

    def _ln_body(tile, pe, dve, act, square=True):
        par = tile % 2
        ti = tile % GTILES
        xf = xf_st[:, par * D:(par + 1) * D]

        def sx(t):
            last = None
            for k in range(NK):
                last = t.matmul(PB[0][0:1, 0:TILE], ones_c[:],
                                xf[:, k * TILE:(k + 1) * TILE],
                                start=(k == 0), stop=(k == NK - 1))
            return last
        pe(sx, reads=[f"xf:{par}", "ones_c"], writes=[BK[0]], tag="sxmm")
        toff0 = ti * TILE
        sq_names = [f"pt:{k}:{ti}" for k in range(NK)]
        if square:
            sqfull = bass.AP(partial.tensor if hasattr(partial, "tensor")
                             else partial, toff0,
                             [[NK * GTOK, 128], [GTOK, NK], [1, TILE]])
            dve(lambda v: v.tensor_tensor(sqfull, xf[:].rearrange(
                "p (k t) -> p k t", k=NK), xf[:].rearrange(
                "p (k t) -> p k t", k=NK), ALU.mult),
                reads=[f"xf:{par}"], writes=sq_names)
        for k in range(NK):
            sqap = partial[:, k * GTOK + toff0:k * GTOK + toff0 + TILE]
            pe(lambda t, k=k, sqap=sqap: t.matmul(
                PB[1][0:1, 0:TILE], ones_c[:], sqap,
                start=(k == 0), stop=(k == NK - 1)),
                reads=[f"pt:{k}:{ti}", "ones_c"], writes=[BK[1]], tag="sxxmm")
        dve(lambda v: v.tensor_scalar_mul(R(0), PB[0][0:1, 0:TILE], 1.0 / D),
            reads=[BK[0]], writes=["srow"])
        dve(lambda v: v.tensor_copy(srow_bf[0:1, 0:128], R(0)),
            reads=["srow"], writes=["srbf"])
        dve(lambda v: v.tensor_tensor(R(1), R(0), R(0), ALU.mult),
            reads=["srow"], writes=["srow"])
        dve(lambda v: v.scalar_tensor_tensor(R(1), PB[1][0:1, 0:TILE],
                                             1.0 / D, R(1),
                                             ALU.mult, ALU.subtract),
            reads=[BK[1], "srow"], writes=["srow"])
        act(lambda s: s.activation(R(2), R(1), AF.Ln, bias=cpeps[0:1, 0:1]),
            reads=["srow", "cpeps"], writes=["srow"])
        act(lambda s: s.activation(R(3), R(2), AF.Exp, scale=-0.5),
            reads=["srow"], writes=["srow"])
        dve(lambda v: v.tensor_copy(srow_bf[0:1, 128:256], R(3)),
            reads=["srow"], writes=["srbf"])
        act(lambda s: s.activation(R(0), R(2), AF.Exp, scale=-1.0),
            reads=["srow"], writes=["srow"])
        dve(lambda v: v.tensor_tensor(R(3), R(1), R(0), ALU.mult),
            reads=["srow"], writes=["srow"])
        dve(lambda v: v.tensor_scalar_add(R(3), R(3), 1.0e-6),
            reads=["srow"], writes=["srow"])
        act(lambda s: s.activation(R(2), R(3), AF.Ln),
            reads=["srow"], writes=["srow"])
        act(lambda s: s.activation(R(0), R(2), AF.Exp, scale=-0.5),
            reads=["srow"], writes=["srow"])
        pe(lambda t: t.transpose(PB[0][:, 128:129],
                                 R(0), ident_f[0:1, 0:1]),
           reads=["srow", "identf"], writes=[BK[0]], tag="alphaT")
        dve(lambda v: v.tensor_copy(acol[:], PB[0][:, 128:129]),
            reads=[BK[0]], writes=["acol"])
        pe(lambda t: t.matmul(PB[0][:, 256:384], ones_r[:],
                              srow_bf[0:1, 128:256], start=True, stop=True),
           reads=["ones_r", "srbf"], writes=[BK[0]], tag="sbc")
        act(lambda s: s.activation(s_sb[:], PB[0][:, 256:384], AF.Copy),
            reads=[BK[0]], writes=["s_sb"])
        toff = ti * TILE
        for k in range(NK):
            bk = k % 2
            pe(lambda t, k=k, bk=bk: (
                t.matmul(PB[bk][:, 384:512], ident[:],
                         xf[:, k * TILE:(k + 1) * TILE],
                         start=True, stop=False),
                t.matmul(PB[bk][:, 384:512], nones_r[:],
                         srow_bf[0:1, 0:128], start=False, stop=True))[-1],
                reads=["ident", f"xf:{par}", "nones_r", "srbf"],
                writes=[BK[bk]], tag="psumx")
            dve(lambda v, k=k, bk=bk, toff=toff: v.tensor_tensor(
                xn_fm[:, k * GTOK + toff:k * GTOK + toff + TILE],
                PB[bk][:, 384:512], s_sb[:], ALU.mult),
                reads=[BK[bk], "s_sb"], writes=[f"xn:{k}:{ti}"], tag="xn")
            dve(lambda v, k=k, toff=toff: v.tensor_scalar(
                xnw_st[:, k * 128:(k + 1) * 128],
                xn_fm[:, k * GTOK + toff:k * GTOK + toff + TILE],
                nw_col[:, k:k + 1], None, ALU.mult),
                reads=[f"xn:{k}:{ti}", "nw_col"], writes=[f"xnw:{k}"],
                tag="xnw")

    def pass_a_qz(tile):
        for k in range(NK):
            xk = xnw_st[:, k * 128:(k + 1) * 128]
            rd = [f"xnw:{k}", "wq", "wgs"]

            def fq(t, k=k, xk=xk):
                t.matmul(PB[0][:, 0:512], xk, wq[:, k * D:k * D + 512],
                         start=(k == 0), stop=(k == NK - 1))
                return t.matmul(PB[1][:, 0:512], xk,
                                wq[:, k * D + 512:(k + 1) * D],
                                start=(k == 0), stop=(k == NK - 1))
            pe(fq, reads=rd, writes=[BK[0], BK[1]], tag="mmQ")
            pe(lambda t, k=k, xk=xk: t.matmul(
                PB[6][:, 0:24], xk, wgs[:, k * 24:(k + 1) * 24],
                start=(k == 0), stop=(k == NK - 1)),
                reads=rd, writes=[BK[6]], tag="mmZ")

    def pass_a_k(tile):
        for k in range(NK):
            xk = xnw_st[:, k * 128:(k + 1) * 128]
            rd = [f"xnw:{k}", "wk"]

            def fk(t, k=k, xk=xk):
                t.matmul(PB[2][:, 0:512], xk, wk[:, k * D:k * D + 512],
                         start=(k == 0), stop=(k == NK - 1))
                return t.matmul(PB[3][:, 0:512], xk,
                                wk[:, k * D + 512:(k + 1) * D],
                                start=(k == 0), stop=(k == NK - 1))
            pe(fk, reads=rd, writes=[BK[2], BK[3]], tag="mmK")

    def pass_a_v(tile):
        for k in range(NK):
            xk = xnw_st[:, k * 128:(k + 1) * 128]
            rd = [f"xnw:{k}", "wv"]

            def fv(t, k=k, xk=xk):
                t.matmul(PB[4][:, 0:512], xk, wv[:, k * D:k * D + 512],
                         start=(k == 0), stop=(k == NK - 1))
                return t.matmul(PB[5][:, 0:512], xk,
                                wv[:, k * D + 512:(k + 1) * D],
                                start=(k == 0), stop=(k == NK - 1))
            pe(fv, reads=rd, writes=[BK[4], BK[5]], tag="mmV")

    def gates_pre(tile):
        dve(lambda v: v.tensor_scalar(zrow[:], PB[6][:, 0:24], acol[:, 0:1],
                                      None, ALU.mult),
            reads=[BK[6], "acol"], writes=["zrow"])
        dve(lambda v: v.tensor_tensor(zrow[:], zrow[:], b24_t[:], ALU.add),
            reads=["zrow", "b24"], writes=["zrow"])
        act(lambda s: s.activation(gsc[:], zrow[:], AF.Exp, scale=-1.0),
            reads=["zrow"], writes=["gscA"])
        act(lambda s: s.activation(gsc[:], gsc[:], AF.Ln, bias=cp1[:, 0:1]),
            reads=["gscA"], writes=["gscA"])
        dve(lambda v: v.tensor_scalar_min(spb[:, 0:8], gsc[:, 0:8], 80.0),
            reads=["gscA"], writes=["spb"])
        dve(lambda v: v.tensor_copy(spb[:, 8:16], gsc[:, 16:24]),
            reads=["gscA"], writes=["spb"])
        dve(lambda v: v.tensor_copy(sp_hi[:], spb[:]),
            reads=["spb"], writes=["sphi"])
        dve(lambda v: v.tensor_tensor(sp_lo[:], spb[:], sp_hi[:],
                                      ALU.subtract),
            reads=["spb", "sphi"], writes=["splo"])

    def gates_mms_a(tile):
        pe(lambda t: (
            t.matmul(PB[6][:, 32:40], tri_bf[:], sp_hi[:, 0:8],
                     start=True, stop=False),
            t.matmul(PB[6][:, 32:40], tri_bf[:], sp_lo[:, 0:8],
                     start=False, stop=True))[-1],
            reads=["tri", "sphi", "splo"], writes=[BK[6]], tag="Pmm")
        dve(lambda v: v.tensor_copy(p_tm[:], PB[6][:, 32:40]),
            reads=[BK[6]], writes=["p_tm"])
        pe(lambda t: (
            t.matmul(PB[7][0:8, 0:128], sp_hi[:, 0:8], tri_bf[:],
                     start=True, stop=False),
            t.matmul(PB[7][0:8, 0:128], sp_lo[:, 0:8], tri_bf[:],
                     start=False, stop=False),
            t.matmul(PB[7][0:8, 0:128], sp_hi[:, 8:16], ident[:],
                     start=False, stop=False),
            t.matmul(PB[7][0:8, 0:128], sp_lo[:, 8:16], ident[:],
                     start=False, stop=True))[-1],
            reads=["tri", "ident", "sphi", "splo"], writes=[BK[7]],
            tag="Pgmm")
        dve(lambda v: v.tensor_copy(pg_hi[:], PB[7][0:8, 0:128]),
            reads=[BK[7]], writes=["pghi"])
        dve(lambda v: v.tensor_tensor(pg_lo[:], PB[7][0:8, 0:128], pg_hi[:],
                                      ALU.subtract),
            reads=[BK[7], "pghi"], writes=["pglo"])
        pe(lambda t: (
            t.matmul(PB[6][0:1, 48:56], ones_c[:], sp_hi[:, 0:8],
                     start=True, stop=False),
            t.matmul(PB[6][0:1, 48:56], ones_c[:], sp_lo[:, 0:8],
                     start=False, stop=True))[-1],
            reads=["ones_c", "sphi", "splo"], writes=[BK[6]], tag="PCmm")
        dve(lambda v: v.tensor_copy(pc_hi[:], PB[6][0:1, 48:56]),
            reads=[BK[6]], writes=["pchi"])
        dve(lambda v: v.tensor_tensor(pc_lo[:], PB[6][0:1, 48:56], pc_hi[:],
                                      ALU.subtract),
            reads=[BK[6], "pchi"], writes=["pclo"])
        dve(lambda v: v.tensor_tensor(b2p[:], p_tm[:], gsc[:, 8:16],
                                      ALU.subtract),
            reads=["p_tm", "gscA"], writes=["b2p"])
        dve(lambda v: v.tensor_scalar_add(b2p[:], b2p[:], LNSCALE),
            reads=["b2p"], writes=["b2p"])

    def gates_mms_b(tile):
        pe(lambda t: (
            t.matmul(PB[6][:, 64:72], ones_r[:], pc_hi[:],
                     start=True, stop=False),
            t.matmul(PB[6][:, 64:72], ones_r[:], pc_lo[:],
                     start=False, stop=True))[-1],
            reads=["ones_r", "pchi", "pclo"], writes=[BK[6]], tag="PCbmm")
        act(lambda s: s.activation(dc_t[:], PB[6][:, 64:72], AF.Exp,
                                   scale=-1.0),
            reads=[BK[6]], writes=["dc"])
        dve(lambda v: v.tensor_tensor(wargt[:], b2p[:], PB[6][:, 64:72],
                                      ALU.subtract),
            reads=["b2p", BK[6]], writes=["wargt"])
        act(lambda s: s.activation(w_tm[:], wargt[:], AF.Exp),
            reads=["wargt"], writes=["w_tm"])

    def evac_q(tile):
        act(lambda s: s.activation(q_tm[:, 0:512], PB[0][:, 0:512], AF.Copy,
                                   scale=acol[:, 0:1]),
            reads=[BK[0], "acol"], writes=["q_tm"])
        act(lambda s: s.activation(q_tm[:, 512:1024], PB[1][:, 0:512],
                                   AF.Copy, scale=acol[:, 0:1]),
            reads=[BK[1], "acol"], writes=["q_tm"])

    def evac_k(tile):
        act(lambda s: s.activation(k_tm[:, 0:512], PB[2][:, 0:512], AF.Copy,
                                   scale=acol[:, 0:1]),
            reads=[BK[2], "acol"], writes=["k_tm"])
        act(lambda s: s.activation(k_tm[:, 512:1024], PB[3][:, 0:512],
                                   AF.Copy, scale=acol[:, 0:1]),
            reads=[BK[3], "acol"], writes=["k_tm"])

    def evac_v(tile):
        dve(lambda v: v.tensor_scalar(v_tm[:, 0:512],
                                      PB[4][:, 0:512], acol[:, 0:1], None,
                                      ALU.mult),
            reads=[BK[4], "acol"], writes=["v_tm"])
        dve(lambda v: v.tensor_scalar(v_tm[:, 512:D],
                                      PB[5][:, 0:512], acol[:, 0:1], None,
                                      ALU.mult),
            reads=[BK[5], "acol"], writes=["v_tm"])

    def _rope_one(tile, src_sb, srcb):
        c = tile % tps
        ti = tile % GTILES
        toff = ti * TILE
        cosap = bass.AP(cos_bf, c * HALF,
                        [[ntab, 128], [0, 4], [0, 2], [1, HALF]])
        sinap = bass.AP(sin_bf, c * HALF,
                        [[ntab, 128], [0, 4], [0, 2], [1, HALF]])
        lo = bass.AP(src_sb, 0, [[D, 128], [256, 4], [128, 2], [1, HALF]])
        hi = bass.AP(src_sb, HALF, [[D, 128], [256, 4], [128, 2], [1, HALF]])
        rsa = bass.AP(partial, toff, [[NK * GTOK, 128], [GTOK, 4],
                                      [HALF, 2], [1, HALF]])
        rsb = bass.AP(partial, 4 * GTOK + toff,
                      [[NK * GTOK, 128], [GTOK, 4], [HALF, 2], [1, HALF]])
        rsc = bass.AP(kw_tm, 0, [[D, 128], [128, 8], [1, HALF]])
        rsd = bass.AP(kw_tm, HALF, [[D, 128], [128, 8], [1, HALF]])
        lo3 = bass.AP(src_sb, 0, [[D, 128], [DH, NH], [1, HALF]])
        hi3 = bass.AP(src_sb, HALF, [[D, 128], [DH, NH], [1, HALF]])
        cos3 = bass.AP(cos_bf, c * HALF, [[ntab, 128], [0, NH], [1, HALF]])
        sin3 = bass.AP(sin_bf, c * HALF, [[ntab, 128], [0, NH], [1, HALF]])
        pt_lo = [f"pt:{k}:{ti}" for k in range(4)]
        pt_hi = [f"pt:{k}:{ti}" for k in range(4, NK)]
        dve(lambda v: v.tensor_tensor(rsa, lo, cosap, ALU.mult),
            reads=[srcb, "cos"], writes=pt_lo)
        dve(lambda v: v.tensor_tensor(rsb, hi, sinap, ALU.mult),
            reads=[srcb, "sin"], writes=pt_hi)
        dve(lambda v: v.tensor_tensor(rsc, lo3, sin3, ALU.mult),
            reads=[srcb, "sin"], writes=["kw"])
        dve(lambda v: v.tensor_tensor(rsd, hi3, cos3, ALU.mult),
            reads=[srcb, "cos"], writes=["kw"])
        dve(lambda v: v.tensor_tensor(lo, rsa, rsb, ALU.subtract),
            reads=pt_lo + pt_hi, writes=[srcb])
        dve(lambda v: v.tensor_tensor(hi3, rsc, rsd, ALU.add),
            reads=["kw"], writes=[srcb])

    def rope_q(tile):
        _rope_one(tile, q_tm, "q_tm")

    def rope_k_core(tile):
        _rope_one(tile, k_tm, "k_tm")

    def issue_tq(tile):
        for h in range(NH):
            r6 = h
            qd = sdma if h % 2 == 0 else adma
            qd(lambda e, h=h, r6=r6: e.dma_start(
                qofm[:, r6 * 128:(r6 + 1) * 128],
                q_tm[:, h * DH:(h + 1) * DH], transpose=True),
                reads=["q_tm"], writes=[f"qofm:{r6}"], tag=f"Tq:{h}")

    def issue_tk(tile):
        for h in range(NH):
            r6 = h
            qd = sdma if h % 2 == 1 else adma
            qd(lambda e, h=h, r6=r6: e.dma_start(
                kfm[:, r6 * 128:(r6 + 1) * 128],
                k_tm[:, h * DH:(h + 1) * DH], transpose=True),
                reads=["k_tm"], writes=[f"kfm:{r6}"], tag=f"Tk:{h}")

    def kw_fold(tile):
        wb = bass.AP(w_tm, 0, [[8, 128], [1, NH], [0, DH]])
        dve(lambda v: v.tensor_tensor(
            kw_tm[:].rearrange("p (h e) -> p h e", e=DH),
            k_tm[:].rearrange("p (h e) -> p h e", e=DH), wb, ALU.mult),
            reads=["k_tm", "w_tm"], writes=["kw"])

    def _sb_slots(h):
        r = h % 4
        r6 = h % 6
        return r, r6, BK[2 + r6], PB[2 + r6]

    def scan_mm(tile, h):
        r, r6, bs, pbs = _sb_slots(h)
        pe(lambda t: (
            t.matmul(pbs[:, 0:128], negsel[:, h * 128:(h + 1) * 128],
                     pg_hi[:], start=True, stop=False),
            t.matmul(pbs[:, 0:128], negsel[:, h * 128:(h + 1) * 128],
                     pg_lo[:], start=False, stop=True))[-1],
            reads=["negsel", "pghi", "pglo"], writes=[bs], tag="Lbc")

    def scan_s(tile, h):
        r, r6, bs, pbs = _sb_slots(h)
        pe(lambda t: t.matmul(pbs[:, 128:256], kfm[:, h * 128:(h + 1) * 128],
                              qofm[:, h * 128:(h + 1) * 128],
                              start=True, stop=True),
           reads=[f"kfm:{h}", f"qofm:{h}"], writes=[bs], tag="S")

    def scan_mneg(tile, h):
        r, r6, bs, pbs = _sb_slots(h)
        act(lambda s: s.activation(dtile[:, r * 128:(r + 1) * 128],
                                   pbs[:, 0:128], AF.Exp),
            reads=[bs], writes=[f"dtile:{r}"])
        dve(lambda v: v.tensor_tensor(pbs[:, 0:128], pbs[:, 0:128],
                                      mneg[:], ALU.add),
            reads=[bs, "mneg"], writes=[bs])

    def scan_exp(tile, h):
        r, r6, bs, pbs = _sb_slots(h)
        seq = tile // tps
        mslot = seq * NH + h
        act(lambda s: s.activation(atile[:, r * 128:(r + 1) * 128],
                                   pbs[:, 0:128], AF.Exp,
                                   bias=b2p[:, h:h + 1]),
            reads=[bs, "b2p"], writes=[f"atile:{r}"])
        gop(lambda g: g.tensor_copy(mbf[:, r * 128:(r + 1) * 128],
                                    msb[:, mslot * 128:(mslot + 1) * 128]),
            reads=[f"M:{mslot}"], writes=[f"mbf:{r}"])

    def scan_shat(tile, h):
        r, r6, bs, pbs = _sb_slots(h)
        dve(lambda v: v.tensor_tensor(shat[:, r * 128:(r + 1) * 128],
                                      pbs[:, 128:256],
                                      atile[:, r * 128:(r + 1) * 128],
                                      ALU.mult),
            reads=[bs, f"atile:{r}"], writes=[f"shat:{r}"])
        gop(lambda g: g.tensor_tensor(qhfm[:, r * 128:(r + 1) * 128],
                                      qofm[:, h * 128:(h + 1) * 128],
                                      dtile[:, r * 128:(r + 1) * 128],
                                      ALU.mult),
            reads=[f"qofm:{h}", f"dtile:{r}"], writes=[f"qhfm:{r}"])

    def scan_y(tile, h):
        r, r6, bs, pbs = _sb_slots(h)
        pe(lambda t: (
            t.matmul(pbs[:, 256:384],
                     v_tm[:, h * DH:(h + 1) * DH],
                     shat[:, r * 128:(r + 1) * 128], start=True, stop=False),
            t.matmul(pbs[:, 256:384], mbf[:, r * 128:(r + 1) * 128],
                     qhfm[:, r * 128:(r + 1) * 128],
                     start=False, stop=True))[-1],
            reads=["v_tm", f"shat:{r}", f"mbf:{r}", f"qhfm:{r}"],
            writes=[bs], tag="Y")
        act(lambda s: s.activation(yfm[:, h * 128:(h + 1) * 128],
                                   pbs[:, 256:384], AF.Copy),
            reads=[bs], writes=[f"y:{h}"])

    def scan_kv(tile, h):
        r, r6, bs, pbs = _sb_slots(h)
        seq = tile // tps
        mslot = seq * NH + h
        msl = msb[:, mslot * 128:(mslot + 1) * 128]
        pe(lambda t: t.matmul(pbs[:, 384:512], kw_tm[:, h * DH:(h + 1) * DH],
                              v_tm[:, h * DH:(h + 1) * DH],
                              start=True, stop=True),
           reads=["kw", "v_tm"], writes=[bs], tag="KV")
        dve(lambda v: v.scalar_tensor_tensor(
            msl, msl, dc_t[:, h:h + 1], pbs[:, 384:512], ALU.mult, ALU.add),
            reads=[f"M:{mslot}", "dc", bs], writes=[f"M:{mslot}"], tag="Mupd")

    def outproj(tile):
        par = tile % 2
        ti = tile % GTILES
        yreads = [f"y:{h}" for h in range(NH)]
        toff = ti * TILE
        for ec in range(NK):
            bk = 2 + ec % 4
            off = 128 * (ec // 4)

            def fn(t, ec=ec, bk=bk, off=off):
                last = None
                for h in range(NH):
                    last = t.matmul(
                        PB[bk][:, off:off + 128],
                        wo[:, h * D + ec * 128:h * D + (ec + 1) * 128],
                        yfm[:, h * 128:(h + 1) * 128],
                        start=(h == 0), stop=(h == NH - 1))
                return last
            pe(fn, reads=["wo"] + yreads, writes=[BK[bk]], tag="outproj")
            dve(lambda v, ec=ec, bk=bk, off=off, toff=toff:
                v.scalar_tensor_tensor(
                    partial[:, ec * GTOK + toff:ec * GTOK + toff + TILE],
                    PB[bk][:, off:off + 128], bout_c[:, ec:ec + 1],
                    xf_st[:, par * D + ec * 128:par * D + (ec + 1) * 128],
                    ALU.add, ALU.add),
                reads=[BK[bk], "bout_c", f"xf:{par}"],
                writes=[f"pt:{ec}:{ti}"], tag="partial")

    def ffn_group(gi):
        xn_reads = [f"xn:{k}:{tt}" for k in range(NK) for tt in range(GTILES)]
        for c2 in range(NC2):
            wpar = c2 % 2
            gdma(lambda g, c2=c2, wpar=wpar: g.dma_start(
                wg_win[:, wpar * D:(wpar + 1) * D].rearrange(
                    "p (k w) -> p k w", k=NK),
                bass.AP(dr["wg_c"].tensor, c2 * 128,
                        [[H2, 128], [H2 * 128, NK], [1, 128]])),
                reads=["wg_c:0", "wg_c:1"],
                writes=[f"wgw:{wpar}"], tag=f"wg:{c2}")
            sdma(lambda g, c2=c2, wpar=wpar: g.dma_start(
                wu_win[:, wpar * D:(wpar + 1) * D].rearrange(
                    "p (k w) -> p k w", k=NK),
                bass.AP(dr["wu_c"].tensor, c2 * 128,
                        [[H2, 128], [H2 * 128, NK], [1, 128]])),
                reads=["wu_c:0", "wu_c:1"],
                writes=[f"wuw:{wpar}"], tag=f"wu:{c2}")
            gb = (0, 1) if c2 % 2 == 0 else (4, 5)
            ub = (2, 3) if c2 % 2 == 0 else (6, 7)

            def fgate(t, wpar=wpar, gb=gb):
                last = None
                for k in range(NK):
                    w = wg_win[:, wpar * D + k * 128:wpar * D + (k + 1) * 128]
                    last = t.matmul(PB[gb[0]][:, 0:512], w,
                                    xn_fm[:, k * GTOK:k * GTOK + 512],
                                    start=(k == 0), stop=(k == NK - 1))
                    last = t.matmul(PB[gb[1]][:, 0:512], w,
                                    xn_fm[:, k * GTOK + 512:(k + 1) * GTOK],
                                    start=(k == 0), stop=(k == NK - 1))
                return last
            pe(fgate, reads=[f"wgw:{wpar}"] + xn_reads,
               writes=[BK[gb[0]], BK[gb[1]]], tag="ffn_gate")

            def fup(t, wpar=wpar, ub=ub):
                last = None
                for k in range(NK):
                    w = wu_win[:, wpar * D + k * 128:wpar * D + (k + 1) * 128]
                    last = t.matmul(PB[ub[0]][:, 0:512], w,
                                    xn_fm[:, k * GTOK:k * GTOK + 512],
                                    start=(k == 0), stop=(k == NK - 1))
                    last = t.matmul(PB[ub[1]][:, 0:512], w,
                                    xn_fm[:, k * GTOK + 512:(k + 1) * GTOK],
                                    start=(k == 0), stop=(k == NK - 1))
                return last
            pe(fup, reads=[f"wuw:{wpar}"] + xn_reads,
               writes=[BK[ub[0]], BK[ub[1]]], tag="ffn_up")
            for half in range(2):
                act(lambda s, c2=c2, gb=gb, half=half: s.activation(
                    hh[:, c2 * GTOK + half * 512:c2 * GTOK + (half + 1) * 512],
                    PB[gb[half]][:, 0:512], AF.Silu, scale=1.0 / 127.0),
                    reads=[BK[gb[half]]], writes=[f"hh:{c2}"], tag="silu")
                dve(lambda v, c2=c2, ub=ub, half=half: v.scalar_tensor_tensor(
                    hh[:, c2 * GTOK + half * 512:c2 * GTOK + (half + 1) * 512],
                    PB[ub[half]][:, 0:512], 1.0 / 127.0,
                    hh[:, c2 * GTOK + half * 512:c2 * GTOK + (half + 1) * 512],
                    ALU.mult, ALU.mult),
                    reads=[BK[ub[half]], f"hh:{c2}"], writes=[f"hh:{c2}"],
                    tag="hhmul")
        seq = (gi * GTILES) // tps
        goff = ((gi * GTILES) % tps) * TILE
        for p in range(2):
            AH = 3

            def _wd_dma(c2, p=p):
                wpar = c2 % 4
                gdma(lambda g, c2=c2, p=p, wpar=wpar: g.dma_start(
                    wd_win[:, wpar * 512:(wpar + 1) * 512],
                    bass.AP(dr["wd_c"].tensor, c2 * 128 * D + p * 512,
                            [[D, 128], [1, 512]])),
                    reads=["wd_c:0", "wd_c:1"],
                    writes=[f"wdw:{wpar}"], tag=f"wd:{p}:{c2}")

            def _wd_mm(c2, p=p):
                wpar = c2 % 4

                def fdown(t, c2=c2, wpar=wpar):
                    last = None
                    for dc in range(4):
                        w = wd_win[:, wpar * 512 + dc * 128:
                                   wpar * 512 + (dc + 1) * 128]
                        last = t.matmul(PB[2 * dc][:, 0:512], w,
                                        hh[:, c2 * GTOK:c2 * GTOK + 512],
                                        start=(c2 == 0), stop=(c2 == NC2 - 1))
                        last = t.matmul(PB[2 * dc + 1][:, 0:512], w,
                                        hh[:, c2 * GTOK + 512:(c2 + 1) * GTOK],
                                        start=(c2 == 0), stop=(c2 == NC2 - 1))
                    return last
                pe(fdown, reads=[f"wdw:{wpar}", f"hh:{c2}"],
                   writes=[BK[i] for i in range(8)], tag="down")

            for c2 in range(NC2 + AH):
                if c2 < NC2:
                    _wd_dma(c2)
                if c2 >= AH:
                    _wd_mm(c2 - AH)
            for dc in range(4):
                ec = p * 4 + dc
                pt_reads = [f"pt:{ec}:{tt}" for tt in range(GTILES)]
                for qr in range(4):
                    sl_o = out_sb[:, (qr % 2) * 256:(qr % 2 + 1) * 256]
                    dve(lambda v, dc=dc, ec=ec, qr=qr, sl_o=sl_o:
                        v.scalar_tensor_tensor(
                            sl_o,
                            PB[2 * dc + qr // 2][:, (qr % 2) * 256:
                                                 (qr % 2 + 1) * 256],
                            1.0 / 127.0,
                            partial[:, ec * GTOK + qr * 256:
                                    ec * GTOK + (qr + 1) * 256],
                            ALU.mult, ALU.add),
                        reads=[BK[2 * dc + qr // 2]] + pt_reads,
                        writes=[f"out_sb:{qr % 2}"], tag="outevac")

                    def fst(sy, ec=ec, seq=seq, goff=goff, qr=qr, sl_o=sl_o):
                        dst = bass.AP(dr["out_fm"].tensor,
                                      (seq * D + ec * 128) * l_seq + goff +
                                      qr * 256,
                                      [[l_seq, 128], [1, 256]])
                        return sy.dma_start(dst, sl_o)
                    sdma(fst, reads=[f"out_sb:{qr % 2}"],
                         tag=f"store:{gi}:{ec}:{qr}")

    loaded = [False] * ntile

    def ensure_load(t):
        if 0 <= t < ntile and not loaded[t]:
            loaded[t] = True
            x_load(t)

    ensure_load(0)
    ensure_load(1)
    ln_stage(0)
    for tile in range(ntile):
        nxt = tile + 1
        ensure_load(nxt)
        pass_a_qz(tile)
        evac_q(tile)
        gates_pre(tile)
        rope_q(tile)
        issue_tq(tile)
        pass_a_k(tile)
        evac_k(tile)
        gates_mms_a(tile)
        if nxt < ntile and nxt % GTILES != 0:
            ln_square(nxt)
        rope_k_core(tile)
        issue_tk(tile)
        pass_a_v(tile)
        gates_mms_b(tile)
        kw_fold(tile)
        evac_v(tile)
        thunks = ln_thunks(nxt) if (nxt < ntile and nxt % GTILES != 0) else []
        nslots = NH + 5
        npc = (len(thunks) + nslots - 1) // nslots if thunks else 0
        idx = 0
        for s in range(nslots):
            if s < NH:
                scan_mm(tile, s)
            if 0 <= s - 1 < NH:
                scan_mneg(tile, s - 1)
            if 0 <= s - 2 < NH:
                scan_exp(tile, s - 2)
            if 0 <= s - 3 < NH:
                scan_shat(tile, s - 3)
            if 0 <= s - 4 < NH:
                scan_y(tile, s - 4)
            if 0 <= s - 5 < NH:
                scan_kv(tile, s - 5)
            for _ in range(npc):
                if idx < len(thunks):
                    thunks[idx]()
                    idx += 1
            if s < NH:
                scan_s(tile, s)
        while idx < len(thunks):
            thunks[idx]()
            idx += 1
        outproj(tile)
        while idx < len(thunks):
            thunks[idx]()
            idx += 1
        if nxt % GTILES == 0:
            ffn_group(tile // GTILES)
            if nxt < ntile:
                ln_stage(nxt)

    with nc.Block() as block:
        def emit(eng_obj, lst):
            for (fn, waits, sem, amt, tag) in lst:
                for (s, v) in waits:
                    eng_obj.wait_ge(s, v)
                ins = fn(eng_obj)
                ins.then_inc(sem, amt)

        @block.tensor
        def _(t):
            emit(t, P.q["pe"])

        @block.vector
        def _(v):
            emit(v, P.q["dve"])

        @block.scalar
        def _(s):
            emit(s, P.q["act"])

        @block.gpsimd
        def _(g):
            emit(g, P.q["gps"])

        @block.sync
        def _(sy):
            emit(sy, P.q["sp"])
            for (s, v) in P.final_waits("sp"):
                sy.wait_ge(s, v)

    return nc


def _layout_inputs(inputs, n_seq, l_seq, n_cores):
    f = np.ascontiguousarray
    x = np.asarray(inputs["L0_attn_x"], dtype=np.float32)
    base = {
        "wq_t": f(np.asarray(inputs["L0_attn_Wq"], np.float32).T),
        "wk_t": f(np.asarray(inputs["L0_attn_Wk"], np.float32).T),
        "wv_t": f(np.asarray(inputs["L0_attn_Wv"], np.float32).T),
        "wout_t": f(np.asarray(inputs["L0_attn_Wout"], np.float32).T),
        "wg_all": f(np.concatenate(
            [np.asarray(inputs["L0_attn_Wbeta"], np.float32).T,
             np.asarray(inputs["L0_attn_Wig"], np.float32).T,
             np.asarray(inputs["L0_attn_Wog"], np.float32).T], axis=1)),
        "b24": f(np.concatenate(
            [np.asarray(inputs["L0_attn_bbeta"], np.float32),
             np.asarray(inputs["L0_attn_big"], np.float32),
             np.asarray(inputs["L0_attn_bog"], np.float32)])[None, :]),
        "bout": f(np.asarray(inputs["L0_attn_bout"], np.float32)[None, :]),
        "nw": f(np.asarray(inputs["L0_attn_norm_w"], np.float32)[None, :]),
        "wgate_t": f(np.asarray(inputs["L0_ffn_w_gate"], np.int32).T),
        "wup_t": f(np.asarray(inputs["L0_ffn_w_up"], np.int32).T),
        "wdown_t": f(np.asarray(inputs["L0_ffn_w_down"], np.int32).T),
    }
    in_maps = []
    for c in range(n_cores):
        xs = x[c * n_seq:(c + 1) * n_seq, :l_seq, :]
        m = dict(base)
        m["x_fm"] = f(xs.transpose(0, 2, 1)).reshape(n_seq * D, l_seq)
        in_maps.append(m)
    return in_maps


def kernel(**inputs):
    n_seq, l_seq, n_cores = 2, 4096, 8
    in_maps = _layout_inputs(inputs, n_seq, l_seq, n_cores)
    nc = build_program(n_seq=n_seq, l_seq=l_seq)
    res = run_bass_kernel_spmd(nc, in_maps, core_ids=list(range(n_cores)))
    outs = []
    for c in range(n_cores):
        o = np.asarray(res.results[c]["out_fm"], dtype=np.float32)
        o = o.reshape(n_seq, D, l_seq).transpose(0, 2, 1)
        outs.append(o)
    return np.ascontiguousarray(np.concatenate(outs, axis=0))
